# version 11
# baseline (speedup 1.0000x reference)
import math
import contextlib
import numpy as np
import ml_dtypes
import concourse.bass as bass
import concourse.mybir as mybir
from concourse.bass_utils import run_bass_kernel_spmd

F32 = mybir.dt.float32
BF16 = mybir.dt.bfloat16
AF = mybir.ActivationFunctionType
ALU = mybir.AluOpType
AX = mybir.AxisListType
bfloat16 = ml_dtypes.bfloat16

ENGINES = ("tensor", "vector", "scalar", "gpsimd", "sync")


def full_cfg():
    return dict(D=4096, T=4096, B=2, NQ=32, NKV=8, C=512, EPS=1e-5, SCALING=2.0)


def derived(cfg):
    d = dict(cfg)
    d["DC"] = cfg["D"] // 128
    d["TOK"] = 2 * cfg["C"]
    d["TI"] = d["TOK"] // 128
    d["TBW"] = min(512, d["TOK"])
    d["TB"] = d["TOK"] // d["TBW"]
    d["OBW"] = min(512, cfg["D"])
    d["OB"] = cfg["D"] // d["OBW"]
    d["KC"] = cfg["NKV"]
    d["CPT"] = cfg["C"] // 128
    d["NCH"] = cfg["T"] // cfg["C"]
    d["SA"] = d["NCH"] // 2 * d["CPT"]
    d["SB"] = d["NCH"] * d["CPT"]
    d["GRP"] = cfg["NQ"] // cfg["NKV"]
    d["SCALE"] = 1.0 / math.sqrt(128.0)
    d["KV_D"] = cfg["NKV"] * 128
    d["NRANK"] = d["NCH"] // 2
    return d


class Sched:

    def __init__(self, nc):
        self.nc = nc
        self.ops = {e: [] for e in ENGINES}
        self.cnt = {}
        self.handles = {}
        self.lastwait = {e: {} for e in ENGINES}

    def new_sem(self, stack, name):
        self.handles[name] = stack.enter_context(self.nc.semaphore(name))
        self.cnt[name] = 0

    def emit(self, eng, fn, waits=(), inc=None, incn=1):
        ws = []
        for (s, v) in waits:
            if v <= 0:
                continue
            if self.lastwait[eng].get(s, -1) >= v:
                continue
            self.lastwait[eng][s] = v
            ws.append((s, v))
        tok = None
        if inc is not None:
            self.cnt[inc] += incn
            tok = (inc, self.cnt[inc])
        self.ops[eng].append((fn, ws, inc, incn))
        return tok

    def dma(self, eng, out, in_, sem, waits=()):
        return self.emit(eng, lambda h, o=out, i=in_: h.dma_start(out=o, in_=i),
                         waits=waits, inc=sem, incn=16)

    def replay(self, eng, handle):
        for (fn, ws, inc, incn) in self.ops[eng]:
            for (s, v) in ws:
                handle.wait_ge(self.handles[s], v)
            inst = fn(handle)
            if inc is not None:
                inst.then_inc(self.handles[inc], incn)


def tmax(*toks):
    out = {}
    for t in toks:
        if t is None:
            continue
        items = t if isinstance(t, list) else [t]
        for (s, v) in items:
            out[s] = max(out.get(s, 0), v)
    return [(s, v) for s, v in out.items()]



def build_program(cfg):
    d = derived(cfg)
    D, T, C = d["D"], cfg["T"], d["C"]
    DC, TOK, TI, TB, TBW = d["DC"], d["TOK"], d["TI"], d["TB"], d["TBW"]
    OB, OBW = d["OB"], d["OBW"]
    NQ, NKV, KC, GRP = d["NQ"], d["NKV"], d["KC"], d["GRP"]
    CPT, NCH, SA, SB = d["CPT"], d["NCH"], d["SA"], d["SB"]
    SCALE, KV_D, NRANK = d["SCALE"], d["KV_D"], d["NRANK"]

    nc = bass.Bass("TRN2", target_bir_lowering=False, debug=False)

    x_in = nc.dram_tensor("x_loc", [TOK, D], F32, kind="ExternalInput")
    cos_in = nc.dram_tensor("rope_cos", [128, TOK], F32, kind="ExternalInput")
    sin_in = nc.dram_tensor("rope_sin", [128, TOK], F32, kind="ExternalInput")
    mask_in = nc.dram_tensor("masks", [SB, 128, C], BF16, kind="ExternalInput")
    wq_in = nc.dram_tensor("wq_s", [NQ, 128, D], F32, kind="ExternalInput")
    wk_in = nc.dram_tensor("wk_s", [KC, 128, D], F32, kind="ExternalInput")
    wv_in = nc.dram_tensor("wv_s", [KC, 128, D], F32, kind="ExternalInput")
    wo_in = nc.dram_tensor("wo_s", [OB, 128, DC * OBW], F32, kind="ExternalInput")
    bw_in = nc.dram_tensor("bw_s", [OB, 128, DC * OBW], F32, kind="ExternalInput")
    lg_in = nc.dram_tensor("lg_s", [128, DC * 136], F32, kind="ExternalInput")
    bfl_in = nc.dram_tensor("bfl_s", [128, D], F32, kind="ExternalInput")
    bias_in = nc.dram_tensor("bias_s", [1, D], F32, kind="ExternalInput")
    cst_in = nc.dram_tensor("consts", [128, 520], F32, kind="ExternalInput")
    sml_in = nc.dram_tensor("smalls", [128, 2], F32, kind="ExternalInput")
    y_out = nc.dram_tensor("y_loc", [TOK, D], F32, kind="ExternalOutput")

    x1d = nc.dram_tensor("x1d", [TOK, D], F32)
    x2d = nc.dram_tensor("x2d", [TOK, D], F32)
    qrd = nc.dram_tensor("qrd", [NQ, 128, TOK], BF16)
    KVR = 2 * KV_D + 1
    kv_in_d = nc.dram_tensor("kv_cc_in", [KVR, TOK], BF16)
    kv_out_d = nc.dram_tensor("kv_cc_out", [NRANK * KVR, TOK], BF16)

    A0 = 0x04400
    B0 = 0x14400
    C0 = 0x24400
    ST = 0x2A400

    def at(name, shape, dtype, off):
        return nc.alloc_sbuf_tensor_at(name, shape, dtype, offset=off)

    lnT = at("lnT", [128, DC * TOK], BF16, A0)
    ctxT = at("ctxT", [128, NQ * TOK], BF16, A0)
    wpanA = at("wpanA", [128, DC * OBW], BF16, B0)
    wpanB = at("wpanB", [128, DC * OBW], BF16, B0 + DC * OBW * 2)
    kpan = [at("kpan0", [128, D], BF16, B0),
            at("kpan1", [128, D], BF16, B0 + D * 2)]
    ksb = at("ksb", [128, T], BF16, B0)
    vtsb = at("vtsb", [128, T], BF16, B0 + T * 2)
    vsb = at("vsb", [128, T], BF16, B0 + 2 * T * 2)
    cossb = at("cossb", [128, TOK], F32, B0 + 3 * T * 2)
    sinsb = at("sinsb", [128, TOK], F32, B0 + 3 * T * 2 + TOK * 4)
    msksb = at("msksb", [128, SB * C], BF16, B0 + 3 * T * 2 + 2 * TOK * 4)
    assert 3 * T * 2 + 2 * TOK * 4 + SB * C * 2 <= 0x10000, "region B overflow"
    xtile = at("xtile", [128, D], F32, C0)
    ntile = at("ntile", [128, D], BF16, C0 + D * 4)
    qsb = [at(f"qsb{i}", [128, TOK], BF16, C0 + i * TOK * 2)
           for i in range(4)]
    PW = 512 + C
    psb = [at(f"psb{i}", [128, PW], BF16, C0 + 4 * TOK * 2 + i * PW * 2)
           for i in range(3)]
    rsb = [at("rsb0", [128, PW], F32, C0 + 4 * TOK * 2 + 3 * PW * 2)]

    cur = [ST]

    def st_alloc(name, shape, dtype):
        nbytes = int(np.prod(shape[1:])) * (2 if dtype == BF16 else 4)
        nbytes = (nbytes + 31) // 32 * 32
        t = at(name, shape, dtype, cur[0])
        cur[0] += nbytes
        return t

    lgsb = st_alloc("lgsb", [128, DC * 136], BF16)
    bflsb = st_alloc("bflsb", [128, D], BF16)
    biasb = st_alloc("biasb", [1, D], BF16)
    slowT = st_alloc("slowT", [128, TOK], BF16)
    wT_sb = st_alloc("wT_sb", [8, TOK], BF16)
    G_sb = st_alloc("G_sb", [128, TBW], BF16)
    stats = st_alloc("stats", [128, TI * 48], F32)
    stg = [st_alloc(f"stg{i}", [128, TBW], BF16) for i in range(4)]
    rtmp = [st_alloc(f"rtmp{i}", [128, TBW], BF16) for i in range(2)]
    sqtl = [st_alloc(f"sqt{i}", [128, TBW], BF16) for i in range(2)]
    xrt = [st_alloc(f"xrt{i}", [128, OBW], F32) for i in range(2)]
    idsb = st_alloc("idsb", [128, 128], BF16)
    e8sb = st_alloc("e8sb", [8, 128], BF16)
    onesr = st_alloc("onesr", [1, 128], BF16)
    onesm = st_alloc("onesm", [128, 128], BF16)
    mv = st_alloc("mv", [128, 2], F32)
    ab_sb = st_alloc("ab_sb", [128, 2], F32)
    gtile = st_alloc("gtile", [128, 8], F32)
    gm_t = st_alloc("gm_t", [128, 8], F32)
    is1_t = st_alloc("is1_t", [128, 8], F32)
    is2_t = st_alloc("is2_t", [128, 8], F32)
    v1_t = st_alloc("v1_t", [128, 1], F32)
    v2_t = st_alloc("v2_t", [128, 1], F32)
    w1_t = st_alloc("w1_t", [128, 1], F32)
    w2_t = st_alloc("w2_t", [128, 1], F32)
    wf_t = st_alloc("wf_t", [128, 8], BF16)
    mqr = st_alloc("mqr", [1, 2 * NQ], F32)
    mkr = st_alloc("mkr", [1, 2 * NKV], F32)
    mqx = st_alloc("mqx", [1, NQ], F32)
    mkx = st_alloc("mkx", [1, NKV], F32)
    mkxb = st_alloc("mkxb", [1, NKV], BF16)
    mkg = st_alloc("mkg", [1, NRANK * NKV], BF16)
    zrow = st_alloc("zrow", [1, TOK], BF16)
    mke = st_alloc("mke", [1, NQ], F32)
    m2r = st_alloc("m2r", [1, NQ], F32)
    negm = st_alloc("negm", [1, NQ], F32)
    negmb = st_alloc("negmb", [1, NQ], BF16)
    smlf = st_alloc("smlf", [128, 2], F32)
    smlb = st_alloc("smlb", [128, 2], BF16)
    biasq = st_alloc("biasq", [128, NQ], F32)
    negp = st_alloc("negp", [1, NQ // 2], BF16)
    ksb2 = at("ksb2", [128, T], BF16, B0 + 3 * T * 2)
    vsb2 = at("vsb2", [128, T], BF16, B0 + T * 2)


    biasq2 = st_alloc("biasq2", [128, NQ // 2], F32)
    assert cur[0] <= 0x37F00, f"sbuf statics overflow: {cur[0]:#x}"

    PS = [nc.alloc_psum_tensor(f"PS{i}", [128, 1024], F32) for i in range(4)]

    class BV:
        def __init__(self, t, off):
            self.t, self.off = t, off

        def __getitem__(self, idx):
            r, c = idx
            return self.t[r, (c.start or 0) + self.off: c.stop + self.off]

    pb = [BV(PS[i // 2], (i % 2) * 512) for i in range(8)]
    Q1OFF = 512
    one_exp = (C == 512)

    stack = contextlib.ExitStack()
    S = Sched(nc)
    for s in ("pe", "ve", "ac", "dcn", "dwa", "dwb", "dwa0", "dwa1",
              "dwb0", "dwb1", "dmx", "dmg", "dxt", "dsg2", "dsg3",
              "dxr0", "dxr1", "dkv", "dms", "dq0", "dq1", "dsg0", "dsg1",
              "dcs", "dkv1", "dkvv", "dkvv1", "gp", "cc1", "cc2"):
        S.new_sem(stack, s)

    MM = lambda out, l, r, st_, sp: (
        lambda h: h.matmul(out, l, r, start=st_, stop=sp))

    toks = [
        S.dma("gpsimd", lgsb[:, :], lg_in[:, :], "dcn"),
        S.dma("gpsimd", bflsb[:, :], bfl_in[:, :], "dcn"),
        S.dma("gpsimd", biasb[:, :], bias_in[:, :], "dcn"),
        S.dma("gpsimd", idsb[:, :], cst_in[:, 0:128], "dcn"),
        S.dma("gpsimd", e8sb[:, :], cst_in[0:8, 128:256], "dcn"),
        S.dma("gpsimd", smlb[:, :], sml_in[:, :], "dcn"),
        S.dma("gpsimd", smlf[:, :], sml_in[:, :], "dcn"),
        S.dma("gpsimd", onesr[:, :], cst_in[0:1, 258:386], "dcn"),
        S.dma("gpsimd", onesm[:, :], cst_in[:, 386:514], "dcn"),
    ]
    t_consts = tmax(*toks)

    def emit_layernorm(src_dram, fused_stats, src_tok_list):
        ev_all = []
        prev_norm = None
        prev_rstd = None
        tile_toks = []
        for ti in range(TI):
            halves = []
            w0 = tmax(src_tok_list[ti] if src_tok_list else None, prev_norm)
            for hh in range(2):
                halves.append(S.dma(
                    "sync", xtile[:, hh * (D // 2):(hh + 1) * (D // 2)],
                    src_dram[ti * 128:(ti + 1) * 128, hh * (D // 2):(hh + 1) * (D // 2)],
                    "dxt", waits=w0))
            if fused_stats is None:
                bt = None
                for p in range(D // 512):
                    bt = S.emit("vector",
                        lambda h, ti=ti, p=p: h.bn_stats(
                            stats[:, ti * 48 + p * 6: ti * 48 + p * 6 + 6],
                            xtile[:, p * 512:(p + 1) * 512]),
                        waits=tmax(halves[0] if p < D // 1024 else halves[1], prev_rstd),
                        inc="ve")
                t_agg = S.emit("vector",
                    lambda h, ti=ti: h.bn_aggr(
                        mv[:, :], stats[:, ti * 48: ti * 48 + 6 * (D // 512)]),
                    waits=tmax(bt), inc="ve")
            else:
                t_agg = S.emit("vector",
                    lambda h, ti=ti: h.bn_aggr(
                        mv[:, :], stats[:, ti * 48: ti * 48 + 6 * (D // 512)]),
                    waits=tmax(fused_stats[ti], prev_rstd, halves[1]), inc="ve")
            t_ln = S.emit("scalar",
                lambda h: h.activation(ab_sb[:, 0:1], mv[:, 1:2], AF.Ln,
                                       bias=smlf[:, 1:2]),
                waits=tmax(t_agg, t_consts), inc="ac")
            t_a = S.emit("scalar",
                lambda h: h.activation(ab_sb[:, 0:1], ab_sb[:, 0:1], AF.Exp,
                                       scale=-0.5),
                waits=tmax(t_ln), inc="ac")
            prev_rstd = t_a
            t_b = S.emit("vector",
                lambda h: h.scalar_tensor_tensor(ab_sb[:, 1:2], mv[:, 0:1], -1.0,
                                                 ab_sb[:, 0:1], ALU.mult, ALU.mult),
                waits=tmax(t_a), inc="ve")
            t_n = S.emit("gpsimd",
                lambda h: h.tensor_scalar(ntile[:, :], xtile[:, :],
                                          ab_sb[:, 0:1], ab_sb[:, 1:2],
                                          ALU.mult, ALU.add),
                waits=tmax(t_b, halves[1]), inc="gp")
            prev_norm = t_n
            tp = []
            for dc in range(DC):
                bank = pb[4 + (dc % 2)]
                wts = [t_n, t_consts]
                if len(tp) >= 2:
                    wts.append(tp[-2])
                t_mm = S.emit("tensor",
                    MM(bank[:, 0:128], ntile[:, dc * 128:(dc + 1) * 128],
                       idsb[:, :], True, True),
                    waits=tmax(*wts), inc="pe")
                if dc % 2 == 0:
                    t_ev = S.emit("vector",
                        lambda h, dc=dc, ti=ti, bank=bank: h.tensor_copy(
                            lnT[:, dc * TOK + ti * 128: dc * TOK + ti * 128 + 128],
                            bank[:, 0:128]),
                        waits=tmax(t_mm), inc="ve")
                else:
                    t_ev = S.emit("scalar",
                        lambda h, dc=dc, ti=ti, bank=bank: h.activation(
                            lnT[:, dc * TOK + ti * 128: dc * TOK + ti * 128 + 128],
                            bank[:, 0:128], AF.Copy),
                        waits=tmax(t_mm), inc="ac")
                tp.append(t_ev)
            ev_all.append(tmax(*tp))
            tile_toks.append(tmax(*tp))
        return tmax(*ev_all), tile_toks

    def emit_ffn(lnT_tok, ln_tiles, res_dram, out_dram, stats_out, pan_init,
                 res_wait):
        low_free = None
        g_free = None
        G_free = None
        slow_done = []
        for tb in range(TB):
            for dc in range(DC):
                t_low = S.emit("tensor",
                    MM(pb[2][:, 0:TBW], lgsb[:, dc * 136: dc * 136 + 128],
                       lnT[:, dc * TOK + tb * TBW: dc * TOK + tb * TBW + TBW],
                       dc == 0, dc == DC - 1),
                    waits=tmax(*(ln_tiles[tb * (TBW // 128):(tb + 1) * (TBW // 128)]),
                               t_consts, low_free) if dc == 0 else (),
                    inc="pe" if dc == DC - 1 else None)
            for tt in range(TBW // 128):
                ti = tb * (TBW // 128) + tt
                for dc in range(DC):
                    t_gmm = S.emit("tensor",
                        MM(pb[3][:, 0:8],
                           lnT[:, dc * TOK + ti * 128: dc * TOK + ti * 128 + 128],
                           lgsb[:, dc * 136 + 128: dc * 136 + 136],
                           dc == 0, dc == DC - 1),
                        waits=tmax(g_free, ln_tiles[ti]) if dc == 0 else (),
                        inc="pe" if dc == DC - 1 else None)
                t_g = S.emit("vector",
                    lambda h: h.tensor_copy(gtile[:, :], pb[3][:, 0:8]),
                    waits=tmax(t_gmm), inc="ve")
                t_v1 = S.emit("vector",
                    lambda h: h.reduce_max(v1_t[:, :], gtile[:, :], axis=AX.X),
                    waits=tmax(t_g), inc="ve")
                t_i1 = S.emit("vector",
                    lambda h: h.tensor_scalar(is1_t[:, :], gtile[:, :],
                                              v1_t[:, :], None, ALU.is_ge),
                    waits=tmax(t_v1), inc="ve")
                t_gm = S.emit("vector",
                    lambda h: h.scalar_tensor_tensor(gm_t[:, :], is1_t[:, :],
                                                     -1e30, gtile[:, :],
                                                     ALU.mult, ALU.add),
                    waits=tmax(t_i1), inc="ve")
                t_v2 = S.emit("vector",
                    lambda h: h.reduce_max(v2_t[:, :], gm_t[:, :], axis=AX.X),
                    waits=tmax(t_gm), inc="ve")
                t_i2 = S.emit("vector",
                    lambda h: h.tensor_scalar(is2_t[:, :], gm_t[:, :],
                                              v2_t[:, :], None, ALU.is_ge),
                    waits=tmax(t_v2), inc="ve")
                t_dd = S.emit("vector",
                    lambda h: h.tensor_sub(v2_t[:, :], v2_t[:, :], v1_t[:, :]),
                    waits=tmax(t_i2), inc="ve")
                t_e = S.emit("scalar",
                    lambda h: h.activation(w1_t[:, :], v2_t[:, :], AF.Exp),
                    waits=tmax(t_dd), inc="ac")
                t_p1 = S.emit("vector",
                    lambda h: h.tensor_scalar(w1_t[:, :], w1_t[:, :], 1.0,
                                              None, ALU.add),
                    waits=tmax(t_e), inc="ve")
                t_r = S.emit("vector",
                    lambda h: h.reciprocal(w1_t[:, :], w1_t[:, :]),
                    waits=tmax(t_p1), inc="ve")
                t_w2 = S.emit("vector",
                    lambda h: h.tensor_scalar(w2_t[:, :], w1_t[:, :], -1.0, 1.0,
                                              ALU.mult, ALU.add),
                    waits=tmax(t_r), inc="ve")
                t_m1 = S.emit("vector",
                    lambda h: h.tensor_scalar(is1_t[:, :], is1_t[:, :],
                                              w1_t[:, :], None, ALU.mult),
                    waits=tmax(t_r), inc="ve")
                t_m2 = S.emit("vector",
                    lambda h: h.tensor_scalar(is2_t[:, :], is2_t[:, :],
                                              w2_t[:, :], None, ALU.mult),
                    waits=tmax(t_w2), inc="ve")
                t_wf = S.emit("vector",
                    lambda h: h.tensor_add(wf_t[:, :], is1_t[:, :], is2_t[:, :]),
                    waits=tmax(t_m1, t_m2), inc="ve")
                t_wmm = S.emit("tensor",
                    MM(pb[3][0:8, 128:256], wf_t[:, :], idsb[:, :], True, True),
                    waits=tmax(t_wf, t_g), inc="pe")
                t_wev = S.emit("vector",
                    lambda h, ti=ti: h.tensor_copy(
                        wT_sb[0:8, ti * 128: ti * 128 + 128], pb[3][0:8, 128:256]),
                    waits=tmax(t_wmm), inc="ve")
                g_free = t_wev
            t_G = S.emit("tensor",
                MM(pb[6][:, 0:TBW], e8sb[:, :],
                   wT_sb[0:8, tb * TBW:(tb + 1) * TBW], True, True),
                waits=tmax(g_free, t_consts, G_free), inc="pe")
            t_Ge = S.emit("vector",
                lambda h: h.tensor_copy(G_sb[:, 0:TBW], pb[6][:, 0:TBW]),
                waits=tmax(t_G), inc="ve")
            G_free = t_Ge
            t_sl = S.emit("vector",
                lambda h, tb=tb: h.tensor_mul(slowT[:, tb * TBW:(tb + 1) * TBW],
                                              pb[2][:, 0:TBW], G_sb[:, 0:TBW]),
                waits=tmax(t_Ge, t_low), inc="ve")
            low_free = t_sl
            slow_done.append(t_sl)
        t_slow = tmax(*slow_done)

        pan_last = list(pan_init)
        ev_toks, stat_toks = [], []
        xr_free = [None, None]
        HW = DC // 2 * OBW
        for ob in range(OB):
            pan = (wpanA, wpanB)[ob % 2]
            pan_tok = [
                S.dma("gpsimd", pan[:, 0:HW], bw_in[ob, :, 0:HW],
                      ("dwa0", "dwb0")[ob % 2], waits=tmax(pan_last[ob % 2])),
                S.dma("gpsimd", pan[:, HW:2 * HW], bw_in[ob, :, HW:2 * HW],
                      ("dwa1", "dwb1")[ob % 2], waits=tmax(pan_last[ob % 2])),
            ]
            for ti in range(TI):
                bank = pb[ti % 2]
                wts = tmax(pan_tok[0], ln_tiles[ti], t_slow, t_consts,
                           ev_toks[-2] if len(ev_toks) >= 2 else None)
                for dc in range(DC):
                    S.emit("tensor",
                        MM(bank[:, 0:OBW],
                           lnT[:, dc * TOK + ti * 128: dc * TOK + ti * 128 + 128],
                           pan[:, dc * OBW:(dc + 1) * OBW], dc == 0, False),
                        waits=wts if dc == 0 else
                        (tmax(pan_tok[1]) if dc == DC // 2 else ()))
                S.emit("tensor",
                    MM(bank[:, 0:OBW], onesr[:, :],
                       biasb[0:1, ob * OBW:(ob + 1) * OBW], False, False))
                t_mm = S.emit("tensor",
                    MM(bank[:, 0:OBW], slowT[:, ti * 128:(ti + 1) * 128],
                       bflsb[:, ob * OBW:(ob + 1) * OBW], False, True),
                    inc="pe")
                pan_last[ob % 2] = t_mm
                u = (ob * TI + ti) % 2
                xr = xrt[u]
                t_xr = S.dma("sync", xr[:, :],
                             res_dram[ti * 128:(ti + 1) * 128,
                                      ob * OBW:(ob + 1) * OBW],
                             ("dxr0", "dxr1")[u], waits=tmax(xr_free[u], res_wait))
                t_ev = S.emit("vector",
                    lambda h, bank=bank, xr=xr: h.scalar_tensor_tensor(
                        xr[:, :], bank[:, 0:OBW], 0.5, xr[:, :],
                        ALU.mult, ALU.add),
                    waits=tmax(t_mm, t_xr), inc="ve")
                t_after = t_ev
                if stats_out:
                    t_after = S.emit("vector",
                        lambda h, ti=ti, ob=ob, xr=xr: h.bn_stats(
                            stats[:, ti * 48 + ob * 6: ti * 48 + ob * 6 + 6],
                            xr[:, :]),
                        waits=tmax(t_ev), inc="ve")
                    stat_toks.append(t_after)
                t_st = S.dma("sync",
                             out_dram[ti * 128:(ti + 1) * 128,
                                      ob * OBW:(ob + 1) * OBW],
                             xr[:, :], ("dxr0", "dxr1")[u], waits=tmax(t_after))
                xr_free[u] = t_st
                ev_toks.append(tmax(t_after, t_st))
        last_ti = [tmax(ev_toks[(OB - 1) * TI + t_]) for t_ in range(TI)]
        return (tmax(*ev_toks),
                last_ti if stats_out else None,
                pan_last)

    t_ln1, ln1_tiles = emit_layernorm(x_in, None, None)
    t_ffn1, t_st1, pan_last1 = emit_ffn(t_ln1, ln1_tiles, x_in, x1d, True,
                                        [None, None], None)

    t_ln2, ln2_tiles = emit_layernorm(x1d, t_st1, t_st1)

    t_cos = S.dma("gpsimd", cossb[:, :], cos_in[:, :], "dcs",
                  waits=tmax(pan_last1[0], pan_last1[1]))
    t_sin = S.dma("gpsimd", sinsb[:, :], sin_in[:, :], "dcs",
                  waits=tmax(pan_last1[0], pan_last1[1]))
    mtok = []
    for s_ in range(SB):
        mtok.append(S.dma("sync", msksb[:, s_ * C:(s_ + 1) * C],
                          mask_in[s_, :, :], "dms",
                          waits=tmax(pan_last1[0], pan_last1[1])))
    t_msk = tmax(*mtok)

    pan_busy = list(pan_last1)
    norm_pending = []

    def flush_norm():
        while norm_pending:
            norm_pending.pop(0)()

    bank_rd = [None] * 4
    nrm_rd = [None, None]
    stg_busy = [None] * 4
    SGS = ("dsg0", "dsg1", "dsg2", "dsg3")
    kv_store = []
    mk_red = []
    uc = 0
    for which, w_in, rope in (("k", wk_in, True), ("v", wv_in, False)):
        for kf in range(KC):
            pan = kpan[kf % 2]
            t_pan = S.dma("gpsimd", pan[:, :], w_in[kf, :, :],
                          ("dwa", "dwb")[kf % 2], waits=tmax(pan_busy[kf % 2]))
            for tb in range(TB):
                bank = pb[uc % 4]
                wts = tmax(t_pan, bank_rd[uc % 4], *ln2_tiles[tb * (TBW // 128):(tb + 1) * (TBW // 128)])
                for dc in range(DC):
                    t_mm = S.emit("tensor",
                        MM(bank[:, 0:TBW], pan[:, dc * 128:(dc + 1) * 128],
                           lnT[:, dc * TOK + tb * TBW: dc * TOK + tb * TBW + TBW],
                           dc == 0, dc == DC - 1),
                        waits=wts if dc == 0 else (),
                        inc="pe" if dc == DC - 1 else None)
                pan_busy[kf % 2] = t_mm
                sg = stg[uc % 4]
                if rope:
                    t_c = S.emit("vector",
                        lambda h, bank=bank, tb=tb: h.tensor_mul(
                            rtmp[0][:, :], bank[:, 0:TBW],
                            cossb[:, tb * TBW:(tb + 1) * TBW]),
                        waits=tmax(t_mm, t_cos, t_sin), inc="ve")
                    t_s1 = S.emit("vector",
                        lambda h, bank=bank, tb=tb: h.tensor_mul(
                            rtmp[1][0:64, :], bank[64:128, 0:TBW],
                            sinsb[0:64, tb * TBW:(tb + 1) * TBW]),
                        inc="ve")
                    t_s2 = S.emit("vector",
                        lambda h, bank=bank, tb=tb: h.tensor_mul(
                            rtmp[1][64:128, :], bank[0:64, 0:TBW],
                            sinsb[64:128, tb * TBW:(tb + 1) * TBW]),
                        inc="ve")
                    t_kr = S.emit("vector",
                        lambda h, sg=sg: h.tensor_add(sg[:, :], rtmp[0][:, :],
                                                      rtmp[1][:, :]),
                        waits=tmax(t_c, t_s1, t_s2, stg_busy[uc % 4]), inc="ve")
                    sq = sqtl[uc % 2]
                    t_q2 = S.emit("vector",
                        lambda h, sg=sg, sq=sq: h.tensor_mul(sq[:, :], sg[:, :],
                                                             sg[:, :]),
                        waits=tmax(t_kr), inc="ve")
                    bank_rd[uc % 4] = t_q2
                    flush_norm()

                    def mk_norm(uc=uc, sq=sq, t_q2=t_q2, kf=kf, tb=tb):
                        nb = pb[6 + (uc % 2)]
                        t_nm = S.emit("tensor",
                            MM(nb[0:1, 0:TBW], smlb[:, 0:1], sq[:, :], True, True),
                            waits=tmax(t_q2, t_consts, nrm_rd[uc % 2]), inc="pe")
                        t_rd = S.emit("vector",
                            lambda h, nb=nb, kf=kf, tb=tb: h.reduce_max(
                                mkr[0:1, kf * TB + tb: kf * TB + tb + 1],
                                nb[0:1, 0:TBW], axis=AX.X),
                            waits=tmax(t_nm), inc="ve")
                        nrm_rd[uc % 2] = t_rd
                        mk_red.append(t_rd)
                    norm_pending.append(mk_norm)
                    t_done = t_q2
                else:
                    t_kr = S.emit("vector",
                        lambda h, bank=bank, sg=sg: h.tensor_copy(
                            sg[:, :], bank[:, 0:TBW]),
                        waits=tmax(t_mm, stg_busy[uc % 4]), inc="ve")
                    bank_rd[uc % 4] = t_kr
                    t_done = t_kr
                row0 = (0 if which == "k" else KV_D) + kf * 128
                t_st = S.dma("sync",
                             kv_in_d[row0:row0 + 128, tb * TBW:(tb + 1) * TBW],
                             sg[:, :], SGS[uc % 4],
                             waits=tmax(t_done))
                stg_busy[uc % 4] = t_st
                kv_store.append(t_st)
                uc += 1

    flush_norm()
    if TB == 2:
        t_mk = S.emit("vector",
            lambda h: h.tensor_max(mkx[0:1, 0:NKV], mkr[0:1, 0:2 * NKV:2],
                                   mkr[0:1, 1:2 * NKV:2]),
            waits=tmax(*mk_red), inc="ve")
    else:
        t_mk = S.emit("vector",
            lambda h: h.tensor_copy(mkx[0:1, 0:NKV], mkr[0:1, 0:NKV]),
            waits=tmax(*mk_red), inc="ve")
    t_mkb16 = S.emit("vector",
        lambda h: h.tensor_scalar(mkxb[0:1, 0:NKV], mkx[0:1, 0:NKV], 1.0,
                                  None, ALU.mult),
        waits=tmax(t_mk), inc="ve")
    t_zr = S.emit("vector", lambda h: h.memset(zrow[0:1, :], 0.0), inc="ve")
    t_zrd = S.dma("gpsimd", kv_in_d[2 * KV_D:2 * KV_D + 1, 0:TOK],
                  zrow[0:1, :], "dmx", waits=tmax(t_zr))
    t_mkd = S.dma("gpsimd", kv_in_d[2 * KV_D:2 * KV_D + 1, 0:NKV],
                  mkxb[0:1, 0:NKV], "dmx", waits=tmax(t_mkb16, t_zrd))

    cc_toks = {}

    def emit_collectives():
        cc_toks["cc1"] = S.emit("gpsimd",
            lambda h: h.collective_compute(
                "AllGather", ALU.bypass,
                replica_groups=[[0, 1, 2, 3], [4, 5, 6, 7]],
                ins=[kv_in_d.ap()], outs=[kv_out_d.ap()]),
            waits=tmax(*kv_store, t_mkd), inc="cc1")

    mq_red = []
    for qc in range(NQ):
        if qc == 2:
            emit_collectives()
        pan = kpan[qc % 2]
        t_pan = S.dma("gpsimd", pan[:, :], wq_in[qc, :, :],
                      ("dwa", "dwb")[qc % 2], waits=tmax(pan_busy[qc % 2]))
        for tb in range(TB):
            bank = pb[uc % 4]
            wts = tmax(t_pan, bank_rd[uc % 4], *ln2_tiles[tb * (TBW // 128):(tb + 1) * (TBW // 128)])
            for dc in range(DC):
                t_mm = S.emit("tensor",
                    MM(bank[:, 0:TBW], pan[:, dc * 128:(dc + 1) * 128],
                       lnT[:, dc * TOK + tb * TBW: dc * TOK + tb * TBW + TBW],
                       dc == 0, dc == DC - 1),
                    waits=wts if dc == 0 else (),
                    inc="pe" if dc == DC - 1 else None)
            pan_busy[qc % 2] = t_mm
            sg = stg[uc % 4]
            t_c = S.emit("vector",
                lambda h, bank=bank, tb=tb: h.tensor_mul(
                    rtmp[0][:, :], bank[:, 0:TBW],
                    cossb[:, tb * TBW:(tb + 1) * TBW]),
                waits=tmax(t_mm, t_cos, t_sin), inc="ve")
            t_s1 = S.emit("vector",
                lambda h, bank=bank, tb=tb: h.tensor_mul(
                    rtmp[1][0:64, :], bank[64:128, 0:TBW],
                    sinsb[0:64, tb * TBW:(tb + 1) * TBW]),
                inc="ve")
            t_s2 = S.emit("vector",
                lambda h, bank=bank, tb=tb: h.tensor_mul(
                    rtmp[1][64:128, :], bank[0:64, 0:TBW],
                    sinsb[64:128, tb * TBW:(tb + 1) * TBW]),
                inc="ve")
            t_kr = S.emit("vector",
                lambda h, sg=sg: h.tensor_add(sg[:, :], rtmp[0][:, :], rtmp[1][:, :]),
                waits=tmax(t_c, t_s1, t_s2, stg_busy[uc % 4]), inc="ve")
            sq = sqtl[uc % 2]
            t_q2 = S.emit("vector",
                lambda h, sg=sg, sq=sq: h.tensor_mul(sq[:, :], sg[:, :], sg[:, :]),
                waits=tmax(t_kr), inc="ve")
            bank_rd[uc % 4] = t_q2
            flush_norm()

            def mq_norm(uc=uc, sq=sq, t_q2=t_q2, qc=qc, tb=tb):
                nb = pb[6 + (uc % 2)]
                t_nm = S.emit("tensor",
                    MM(nb[0:1, 0:TBW], smlb[:, 0:1], sq[:, :], True, True),
                    waits=tmax(t_q2, t_consts, nrm_rd[uc % 2]), inc="pe")
                t_rd = S.emit("vector",
                    lambda h, nb=nb, qc=qc, tb=tb: h.reduce_max(
                        mqr[0:1, qc * TB + tb: qc * TB + tb + 1],
                        nb[0:1, 0:TBW], axis=AX.X),
                    waits=tmax(t_nm), inc="ve")
                nrm_rd[uc % 2] = t_rd
                mq_red.append(t_rd)
            norm_pending.append(mq_norm)
            t_st = S.dma("sync", qrd[qc, :, tb * TBW:(tb + 1) * TBW], sg[:, :],
                         SGS[uc % 4], waits=tmax(t_q2))
            stg_busy[uc % 4] = t_st
            uc += 1

    flush_norm()
    t_qst = [(s_, S.cnt[s_]) for s_ in SGS]
    if TB == 2:
        t_mq = S.emit("vector",
            lambda h: h.tensor_max(mqx[0:1, 0:NQ], mqr[0:1, 0:2 * NQ:2],
                                   mqr[0:1, 1:2 * NQ:2]),
            waits=tmax(*mq_red), inc="ve")
    else:
        t_mq = S.emit("vector",
            lambda h: h.tensor_copy(mqx[0:1, 0:NQ], mqr[0:1, 0:NQ]),
            waits=tmax(*mq_red), inc="ve")
    t_mg = S.dma("sync", mkg[0:1, 0:NKV],
                 kv_out_d[2 * KV_D:2 * KV_D + 1, 0:NKV],
                 "dmg", waits=tmax(cc_toks["cc1"]))
    mg_toks = [t_mg]
    for r_ in range(1, NRANK):
        mg_toks.append(S.dma(
            "sync", mkg[0:1, r_ * NKV:(r_ + 1) * NKV],
            kv_out_d[r_ * KVR + 2 * KV_D:r_ * KVR + 2 * KV_D + 1, 0:NKV],
            "dmg", waits=()))
    t_mx = None
    for r_ in range(1, NRANK):
        t_mx = S.emit("vector",
            lambda h, r_=r_: h.tensor_tensor(
                mkg[0:1, 0:NKV], mkg[0:1, 0:NKV],
                mkg[0:1, r_ * NKV:(r_ + 1) * NKV], ALU.max),
            waits=tmax(*mg_toks, t_mx), inc="ve")
    t_mkb = t_mx
    t_me = None
    for r in range(GRP):
        t_me = S.emit("vector",
            lambda h, r=r: h.tensor_copy(mke[0:1, r:NQ:GRP], mkg[0:1, 0:NKV]),
            waits=tmax(t_mkb, t_me), inc="ve")
    t_m2 = S.emit("vector",
        lambda h: h.tensor_mul(m2r[0:1, 0:NQ], mqx[0:1, 0:NQ], mke[0:1, 0:NQ]),
        waits=tmax(t_mq, t_me), inc="ve")
    t_l2 = S.emit("scalar",
        lambda h: h.activation(m2r[0:1, 0:NQ], m2r[0:1, 0:NQ], AF.Ln),
        waits=tmax(t_m2), inc="ac")
    t_ex = S.emit("scalar",
        lambda h: h.activation(negm[0:1, 0:NQ], m2r[0:1, 0:NQ], AF.Exp, scale=0.5),
        waits=tmax(t_l2), inc="ac")
    t_ng = S.emit("vector",
        lambda h: h.tensor_scalar(negmb[0:1, 0:NQ], negm[0:1, 0:NQ],
                                  -SCALE * 1.02, None, ALU.mult),
        waits=tmax(t_ex), inc="ve")
    t_bq = S.emit("tensor",
        MM(pb[5][:, 0:NQ], onesr[:, :], negmb[0:1, 0:NQ], True, True),
        waits=tmax(t_ng, t_consts), inc="pe")
    t_bqe = S.emit("vector",
        lambda h: h.tensor_copy(biasq[:, :], pb[5][:, 0:NQ]),
        waits=tmax(t_bq), inc="ve")
    t_np = S.emit("vector",
        lambda h: h.tensor_tensor(negp[0:1, 0:NQ // 2], negmb[0:1, 0:NQ:2],
                                  negmb[0:1, 1:NQ:2], ALU.min),
        waits=tmax(t_ng), inc="ve")
    t_bq2 = S.emit("tensor",
        MM(pb[4][:, 0:NQ // 2], onesr[:, :], negp[0:1, 0:NQ // 2], True, True),
        waits=tmax(t_np, t_bqe), inc="pe")
    t_bqe2 = S.emit("vector",
        lambda h: h.tensor_copy(biasq2[:, :], pb[4][:, 0:NQ // 2]),
        waits=tmax(t_bq2), inc="ve")
    t_bqe = tmax(t_bqe, t_bqe2)

    kv_ready = tmax(cc_toks["cc1"])
    ctx_ev = []
    q_busy = [tmax(t_ln2), tmax(t_ln2)]
    p_busy = [tmax(t_ln2), tmax(t_ln2)]
    r_busy = [tmax(t_ln2), tmax(t_ln2)]
    def q0_of(u_pair):
        return (u_pair // (GRP // 2)) * GRP + 2 * (u_pair % (GRP // 2))

    NPAIR = NKV * (GRP // 2)
    qtok = {}

    def emit_q_load(u_pair):
        if u_pair >= NPAIR or u_pair in qtok:
            return
        us_ = u_pair % 2
        t0_ = S.dma("scalar", qsb[2 * us_][:, :], qrd[q0_of(u_pair), :, :],
                    ("dq0", "dq1")[us_], waits=tmax(q_busy[us_], t_qst))
        t1_ = S.dma("scalar", qsb[2 * us_ + 1][:, :],
                    qrd[q0_of(u_pair) + 1, :, :],
                    ("dq0", "dq1")[us_], waits=tmax(q_busy[us_], t_qst))
        qtok[u_pair] = tmax(t0_, t1_)

    kvset = [(ksb, vsb, "dkv"), (ksb2, vsb2, "dkv1")]
    kv_busy = [None, None]
    kv_load = [None, None]

    def emit_kv_load(h_kv):
        kbuf, vbuf, sem = kvset[h_kv % 2]
        vsem = ("dkvv", "dkvv1")[h_kv % 2]
        kgate = tmax(kv_ready, kv_busy[h_kv % 2],
                     pan_busy[0] if h_kv % 2 == 0 else t_bqe)
        vgate = tmax(kv_ready, kv_busy[h_kv % 2],
                     None if h_kv % 2 == 0 else pan_busy[1])
        k_toks = []
        for ch in range(NCH):
            jo = min(ch, NCH - 1 - ch)
            side = 0 if ch == jo else 1
            srk = jo * KVR + h_kv * 128
            srv = jo * KVR + KV_D + h_kv * 128
            t1 = S.dma("sync", kbuf[:, ch * C:(ch + 1) * C],
                       kv_out_d[srk:srk + 128, side * C: side * C + C],
                       sem, waits=kgate)
            k_toks.append(t1)
            for tt in range(CPT):
                t2 = S.emit("sync",
                    lambda h, vbuf=vbuf, ch=ch, tt=tt, srv=srv, side=side:
                    h.dma_start_transpose(
                        vbuf[:, (ch * CPT + tt) * 128:(ch * CPT + tt) * 128 + 128],
                        kv_out_d[srv:srv + 128,
                                 side * C + tt * 128: side * C + tt * 128 + 128]),
                    waits=vgate, inc=vsem, incn=16)
                k_toks.append(t2)
        kv_load[h_kv % 2] = tmax(*k_toks)

    emit_kv_load(0)
    for h_kv in range(NKV):
        ksb_c, vsb_c, _ = kvset[h_kv % 2]
        t_kload = kv_load[h_kv % 2]
        vt_done = t_kload
        if h_kv + 1 < NKV:
            emit_kv_load(h_kv + 1)

        ksb = ksb_c
        vsb = vsb_c
        for pr in range(GRP // 2):
            u_pair = h_kv * (GRP // 2) + pr
            us = u_pair % 2
            qA, qB = qsb[2 * us], qsb[2 * us + 1]
            emit_q_load(u_pair)
            q0 = h_kv * GRP + 2 * pr
            q1 = q0 + 1
            t_q = tmax(qtok[u_pair], t_bqe)
            for cki, nslot in ((0, SA), (1, SB)):
                qoff = cki * C
                exp_toks = []
                pm_toks = []
                sc_toks = []

                def emit_scores(s):
                    SC = PS[s % 2]
                    wts = tmax(t_kload, t_q,
                               exp_toks[s - 2] if s >= 2 else None)
                    if s == 0 and not ctx_ev:
                        wts = tmax(wts, t_bqe)
                    S.emit("tensor",
                        MM(SC[:, 0:C], ksb[:, s * 128: s * 128 + 128],
                           qA[:, qoff: qoff + C], True, True),
                        waits=wts)
                    t = S.emit("tensor",
                        MM(SC[:, Q1OFF:Q1OFF + C], ksb[:, s * 128: s * 128 + 128],
                           qB[:, qoff: qoff + C], True, True),
                        inc="pe")
                    sc_toks.append(t)
                    pt = psb[s % 3]
                    pw = tmax(t, sc_toks[s - 2] if s >= 2 else None)
                    if one_exp:
                        t_e = S.emit("scalar",
                            lambda h, SC=SC, pt=pt: h.activation(
                                pt[:, 0:2 * C], SC[:, 0:2 * C], AF.Exp,
                                bias=biasq2[:, q0 // 2:q0 // 2 + 1], scale=SCALE),
                            waits=pw, inc="ac")
                    else:
                        S.emit("scalar",
                            lambda h, SC=SC, pt=pt: h.activation(
                                pt[:, 0:C], SC[:, 0:C], AF.Exp,
                                bias=biasq[:, q0:q0 + 1], scale=SCALE),
                            waits=pw, inc="ac")
                        t_e = S.emit("scalar",
                            lambda h, SC=SC, pt=pt: h.activation(
                                pt[:, Q1OFF:Q1OFF + C], SC[:, Q1OFF:Q1OFF + C],
                                AF.Exp, bias=biasq[:, q1:q1 + 1], scale=SCALE),
                            inc="ac")
                    exp_toks.append(t_e)
                    masked = (cki == 0) or (s >= SA)
                    if masked:
                        S.emit("vector",
                            lambda h, pt=pt, s=s: h.tensor_mul(
                                pt[:, 0:C], pt[:, 0:C],
                                msksb[:, s * C:(s + 1) * C]),
                            waits=tmax(t_e, t_msk), inc="ve")
                        t_p = S.emit("vector",
                            lambda h, pt=pt, s=s: h.tensor_mul(
                                pt[:, Q1OFF:Q1OFF + C], pt[:, Q1OFF:Q1OFF + C],
                                msksb[:, s * C:(s + 1) * C]),
                            inc="ve")
                        pm_toks.append(t_p)
                    else:
                        pm_toks.append(t_e)

                def emit_accum(s):
                    pt = psb[s % 3]
                    wts = tmax(pm_toks[s], t_consts)
                    if s == 0:
                        wts = tmax(wts, ctx_ev[-1] if ctx_ev else None, vt_done)
                    S.emit("tensor",
                        MM(PS[3][:, 0:C], onesm[:, :], pt[:, 0:C],
                           s == 0, s == nslot - 1),
                        waits=wts)
                    S.emit("tensor",
                        MM(PS[3][:, 512:512 + C], onesm[:, :],
                           pt[:, Q1OFF:Q1OFF + C], s == 0, s == nslot - 1))
                    S.emit("tensor",
                        MM(PS[2][:, 0:C], vsb[:, s * 128: s * 128 + 128],
                           pt[:, 0:C], s == 0, s == nslot - 1))
                    return S.emit("tensor",
                        MM(PS[2][:, 512:512 + C], vsb[:, s * 128: s * 128 + 128],
                           pt[:, Q1OFF:Q1OFF + C], s == 0, s == nslot - 1),
                        inc="pe")

                last_pe = None
                for s in range(nslot):
                    emit_scores(s)
                    if s >= 2:
                        last_pe = emit_accum(s - 2)
                last_pe = emit_accum(nslot - 2) if nslot >= 2 else last_pe
                last_pe = emit_accum(nslot - 1)

                if one_exp:
                    t_r0 = S.emit("vector",
                        lambda h: h.reciprocal(rsb[0][:, 0:512 + C],
                                               PS[3][:, 0:512 + C]),
                        waits=tmax(last_pe, r_busy[0], r_busy[1]), inc="ve")
                    t_r1 = t_r0
                else:
                    t_r0 = S.emit("vector",
                        lambda h: h.reciprocal(rsb[0][:, 0:C], PS[3][:, 0:C]),
                        waits=tmax(last_pe, r_busy[0], r_busy[1]), inc="ve")
                    t_r1 = S.emit("vector",
                        lambda h: h.reciprocal(rsb[0][:, 512:512 + C],
                                               PS[3][:, 512:512 + C]),
                        inc="ve")
                t_x0 = S.emit("vector",
                    lambda h, q0=q0, qoff=qoff: h.tensor_mul(
                        ctxT[:, q0 * TOK + qoff: q0 * TOK + qoff + C],
                        PS[2][:, 0:C], rsb[0][:, 0:C]),
                    waits=tmax(t_r0), inc="ve")
                t_x1 = S.emit("vector",
                    lambda h, q1=q1, qoff=qoff: h.tensor_mul(
                        ctxT[:, q1 * TOK + qoff: q1 * TOK + qoff + C],
                        PS[2][:, 512:512 + C], rsb[0][:, 512:512 + C]),
                    waits=tmax(t_r1), inc="ve")
                r_busy[0], r_busy[1] = t_x0, t_x1
                ctx_ev.append(tmax(t_x0, t_x1))
                if cki == 0:
                    emit_q_load(u_pair + 1)
            q_busy[us] = ctx_ev[-1]
        kv_busy[h_kv % 2] = ctx_ev[-1]
    t_ctx = tmax(*ctx_ev)

    ev3, st3 = [], []
    pan_last3 = [None, None]
    xr_free = [tmax(t_ctx), tmax(t_ctx)]
    HW3 = DC // 2 * OBW
    for ob in range(OB):
        pan = (wpanA, wpanB)[ob % 2]
        pan_tok3 = [
            S.dma("gpsimd", pan[:, 0:HW3], wo_in[ob, :, 0:HW3],
                  ("dwa0", "dwb0")[ob % 2],
                  waits=tmax(pan_last3[ob % 2], t_ctx)),
            S.dma("gpsimd", pan[:, HW3:2 * HW3], wo_in[ob, :, HW3:2 * HW3],
                  ("dwa1", "dwb1")[ob % 2],
                  waits=tmax(pan_last3[ob % 2], t_ctx)),
        ]
        for ti in range(TI):
            bank = pb[ti % 2]
            wts = tmax(pan_tok3[0], t_ctx, ev3[-2] if len(ev3) >= 2 else None)
            for dc in range(DC):
                t_mm = S.emit("tensor",
                    MM(bank[:, 0:OBW],
                       ctxT[:, dc * TOK + ti * 128: dc * TOK + ti * 128 + 128],
                       pan[:, dc * OBW:(dc + 1) * OBW], dc == 0, dc == DC - 1),
                    waits=wts if dc == 0 else
                    (tmax(pan_tok3[1]) if dc == DC // 2 else ()),
                    inc="pe" if dc == DC - 1 else None)
            pan_last3[ob % 2] = t_mm
            u = (ob * TI + ti) % 2
            xr = xrt[u]
            t_xr = S.dma("sync", xr[:, :],
                         x1d[ti * 128:(ti + 1) * 128, ob * OBW:(ob + 1) * OBW],
                         ("dxr0", "dxr1")[u], waits=tmax(xr_free[u]))
            t_ev = S.emit("vector",
                lambda h, bank=bank, xr=xr: h.tensor_add(
                    xr[:, :], bank[:, 0:OBW], xr[:, :]),
                waits=tmax(t_mm, t_xr), inc="ve")
            t_bs = S.emit("vector",
                lambda h, ti=ti, ob=ob, xr=xr: h.bn_stats(
                    stats[:, ti * 48 + ob * 6: ti * 48 + ob * 6 + 6], xr[:, :]),
                waits=tmax(t_ev), inc="ve")
            st3.append(t_bs)
            t_st = S.dma("sync",
                         x2d[ti * 128:(ti + 1) * 128, ob * OBW:(ob + 1) * OBW],
                         xr[:, :], ("dxr0", "dxr1")[u], waits=tmax(t_bs))
            xr_free[u] = t_st
            ev3.append(tmax(t_bs, t_st))
    t_x2 = tmax(*ev3)
    x2_tiles = [tmax(ev3[(OB - 1) * TI + t_]) for t_ in range(TI)]

    t_ln3, ln3_tiles = emit_layernorm(x2d, x2_tiles, x2_tiles)
    t_y, _, _ = emit_ffn(t_ln3, ln3_tiles, x2d, y_out, False, pan_last3, None)

    S.emit("sync", lambda h: h.nop(),
           waits=tmax(t_y, ("dxr0", S.cnt["dxr0"]), ("dxr1", S.cnt["dxr1"])))

    with nc.Block() as block:
        @block.tensor
        def _(h):
            S.replay("tensor", h)

        @block.vector
        def _(h):
            S.replay("vector", h)

        @block.scalar
        def _(h):
            S.replay("scalar", h)

        @block.gpsimd
        def _(h):
            S.replay("gpsimd", h)

        @block.sync
        def _(h):
            S.replay("sync", h)

    stack.close()
    return nc



def stage_weights(cfg, Wq, Wk, Wv, Wo, gate_w, base_w, base_b, lora_A, lora_B):
    d = derived(cfg)
    D, DC, OB, OBW = cfg["D"], d["DC"], d["OB"], d["OBW"]
    NQ, KC = cfg["NQ"], d["KC"]
    f32 = np.float32
    wq_s = np.ascontiguousarray(
        Wq.reshape(DC, 128, NQ, 128).transpose(2, 1, 0, 3).reshape(NQ, 128, D), f32)
    wk_s = np.ascontiguousarray(
        Wk.reshape(DC, 128, KC, 128).transpose(2, 1, 0, 3).reshape(KC, 128, D), f32)
    wv_s = np.ascontiguousarray(
        Wv.reshape(DC, 128, KC, 128).transpose(2, 1, 0, 3).reshape(KC, 128, D), f32)
    wo_s = np.ascontiguousarray(
        Wo.reshape(DC, 128, OB, OBW).transpose(2, 1, 0, 3).reshape(OB, 128, DC * OBW), f32)
    bw_s = np.ascontiguousarray(
        base_w.T.reshape(DC, 128, OB, OBW).transpose(2, 1, 0, 3)
        .reshape(OB, 128, DC * OBW), f32)
    A_flat = lora_A.reshape(128, D)
    lg_s = np.ascontiguousarray(
        np.concatenate([A_flat.T.reshape(DC, 128, 128),
                        gate_w.T.reshape(DC, 128, 8)], axis=2)
        .transpose(1, 0, 2).reshape(128, DC * 136), f32)
    bfl_s = np.ascontiguousarray(lora_B.transpose(0, 2, 1).reshape(128, D), f32)
    bias_s = np.ascontiguousarray(base_b.reshape(1, D), f32)
    consts = np.zeros((128, 520), f32)
    consts[:, 0:128] = np.eye(128, dtype=f32)
    for e in range(8):
        consts[e, 128 + e * 16:128 + (e + 1) * 16] = float(cfg["SCALING"])
    consts[0, 258:386] = 1.0
    consts[:, 386:514] = 1.0
    smalls = np.zeros((128, 2), f32)
    smalls[:, 0] = 1.0
    smalls[:, 1] = float(cfg["EPS"])
    return dict(wq_s=wq_s, wk_s=wk_s, wv_s=wv_s, wo_s=wo_s, bw_s=bw_s,
                lg_s=lg_s, bfl_s=bfl_s, bias_s=bias_s, consts=consts,
                smalls=smalls)


def rope_tables(positions):
    ROPE = dict(rope_theta=500000.0, factor=32.0, hi_freq_factor=4.0,
                lo_freq_factor=1.0, original_context_length=8192)
    idx = np.arange(0, 128, 2, dtype=np.float64) / 128.0
    freq = 1.0 / (2.0 * math.pi) * ROPE["rope_theta"] ** (-idx)
    factor, lo, hi = ROPE["factor"], ROPE["lo_freq_factor"], ROPE["hi_freq_factor"]
    L0 = ROPE["original_context_length"]
    freq_low, freq_high = lo / L0, hi / L0
    freq_scaled = np.where(freq < freq_low, freq / factor, freq)
    smooth = np.clip((L0 * freq - lo) / (hi - lo), 0.0, 1.0)
    freq_smooth = (1.0 - smooth) * (freq / factor) + smooth * freq
    is_mid = (freq >= freq_low) & (freq <= freq_high)
    freq = np.where(is_mid, freq_smooth, freq_scaled)
    phase = 2.0 * math.pi * positions.astype(np.float64)[:, None] * freq[None, :]
    cos = np.cos(phase)
    sin = np.sin(phase)
    cos_t = np.concatenate([cos, cos], axis=1).T
    sin_signed = np.concatenate([-sin, sin], axis=1).T
    return (np.ascontiguousarray(cos_t, np.float32),
            np.ascontiguousarray(sin_signed, np.float32))


def build_masks(cfg, j):
    d = derived(cfg)
    C, CPT, NCH, SA, SBn = d["C"], d["CPT"], d["NCH"], d["SA"], d["SB"]
    kk = np.arange(128)[:, None]
    qq = np.arange(C)[None, :]
    tri = [(qq >= dd * 128 + kk).astype(np.float32) for dd in range(CPT)]
    ones = np.ones((128, C), np.float32)
    zeros = np.zeros((128, C), np.float32)
    m = np.zeros((SBn, 128, C), np.float32)
    for s in range(SA):
        m[s] = ones if s < CPT * j else (
            tri[s - CPT * j] if s < CPT * (j + 1) else zeros)
    cB = NCH - 1 - j
    for s in range(SA, SBn):
        m[s] = ones if s < CPT * cB else (
            tri[s - CPT * cB] if s < CPT * (cB + 1) else zeros)
    return m.astype(bfloat16)


def stage_core_inputs(cfg, c, x, ws):
    d = derived(cfg)
    C, NCH = d["C"], d["NCH"]
    b, j = c // (NCH // 2), c % (NCH // 2)
    pos = np.concatenate([np.arange(C * j, C * (j + 1)),
                          np.arange(C * (NCH - 1 - j), C * (NCH - j))])
    x_loc = np.ascontiguousarray(x[b, pos, :], np.float32)
    cos_t, sin_t = rope_tables(pos)
    m = dict(ws)
    m.update(x_loc=x_loc, rope_cos=cos_t, rope_sin=sin_t,
             masks=build_masks(cfg, j))
    return m


def assemble_output(cfg, results):
    d = derived(cfg)
    C, NCH = d["C"], d["NCH"]
    out = np.zeros((cfg["B"], cfg["T"], cfg["D"]), np.float32)
    for c in range(cfg["B"] * (NCH // 2)):
        b, j = c // (NCH // 2), c % (NCH // 2)
        y = results[c]["y_loc"]
        out[b, C * j:C * (j + 1), :] = y[0:C]
        out[b, C * (NCH - 1 - j):C * (NCH - j), :] = y[C:2 * C]
    return out


def kernel(L0_attn_resid, L0_attn_Wq, L0_attn_Wk, L0_attn_Wv, L0_attn_Wo,
           L0_ffn_gate_w, L0_ffn_base_w, L0_ffn_base_b, L0_ffn_lora_A,
           L0_ffn_lora_B):
    cfg = full_cfg()
    ws = stage_weights(cfg, np.asarray(L0_attn_Wq), np.asarray(L0_attn_Wk),
                       np.asarray(L0_attn_Wv), np.asarray(L0_attn_Wo),
                       np.asarray(L0_ffn_gate_w), np.asarray(L0_ffn_base_w),
                       np.asarray(L0_ffn_base_b), np.asarray(L0_ffn_lora_A),
                       np.asarray(L0_ffn_lora_B))
    x = np.asarray(L0_attn_resid)
    in_maps = [stage_core_inputs(cfg, c, x, ws) for c in range(8)]
    nc = build_program(cfg)
    res = run_bass_kernel_spmd(nc, in_maps, core_ids=list(range(8)))
    return assemble_output(cfg, res.results)


# version 12
# speedup vs baseline: 1.0286x; 1.0286x over previous
import math
import contextlib
import numpy as np
import ml_dtypes
import concourse.bass as bass
import concourse.mybir as mybir
from concourse.bass_utils import run_bass_kernel_spmd

F32 = mybir.dt.float32
BF16 = mybir.dt.bfloat16
AF = mybir.ActivationFunctionType
ALU = mybir.AluOpType
AX = mybir.AxisListType
bfloat16 = ml_dtypes.bfloat16

ENGINES = ("tensor", "vector", "scalar", "gpsimd", "sync")


def full_cfg():
    return dict(D=4096, T=4096, B=2, NQ=32, NKV=8, C=512, EPS=1e-5, SCALING=2.0)


def derived(cfg):
    d = dict(cfg)
    d["DC"] = cfg["D"] // 128
    d["TOK"] = 2 * cfg["C"]
    d["TI"] = d["TOK"] // 128
    d["TBW"] = min(512, d["TOK"])
    d["TB"] = d["TOK"] // d["TBW"]
    d["OBW"] = min(512, cfg["D"])
    d["OB"] = cfg["D"] // d["OBW"]
    d["KC"] = cfg["NKV"]
    d["CPT"] = cfg["C"] // 128
    d["NCH"] = cfg["T"] // cfg["C"]
    d["SA"] = d["NCH"] // 2 * d["CPT"]
    d["SB"] = d["NCH"] * d["CPT"]
    d["GRP"] = cfg["NQ"] // cfg["NKV"]
    d["SCALE"] = 1.0 / math.sqrt(128.0)
    d["KV_D"] = cfg["NKV"] * 128
    d["NRANK"] = d["NCH"] // 2
    return d


class Sched:

    def __init__(self, nc):
        self.nc = nc
        self.ops = {e: [] for e in ENGINES}
        self.cnt = {}
        self.handles = {}
        self.lastwait = {e: {} for e in ENGINES}

    def new_sem(self, stack, name):
        self.handles[name] = stack.enter_context(self.nc.semaphore(name))
        self.cnt[name] = 0

    def emit(self, eng, fn, waits=(), inc=None, incn=1):
        ws = []
        for (s, v) in waits:
            if v <= 0:
                continue
            if self.lastwait[eng].get(s, -1) >= v:
                continue
            self.lastwait[eng][s] = v
            ws.append((s, v))
        tok = None
        if inc is not None:
            self.cnt[inc] += incn
            tok = (inc, self.cnt[inc])
        self.ops[eng].append((fn, ws, inc, incn))
        return tok

    def dma(self, eng, out, in_, sem, waits=()):
        return self.emit(eng, lambda h, o=out, i=in_: h.dma_start(out=o, in_=i),
                         waits=waits, inc=sem, incn=16)

    def replay(self, eng, handle):
        for (fn, ws, inc, incn) in self.ops[eng]:
            for (s, v) in ws:
                handle.wait_ge(self.handles[s], v)
            inst = fn(handle)
            if inc is not None:
                inst.then_inc(self.handles[inc], incn)


def tmax(*toks):
    out = {}
    for t in toks:
        if t is None:
            continue
        items = t if isinstance(t, list) else [t]
        for (s, v) in items:
            out[s] = max(out.get(s, 0), v)
    return [(s, v) for s, v in out.items()]



def build_program(cfg):
    d = derived(cfg)
    D, T, C = d["D"], cfg["T"], d["C"]
    DC, TOK, TI, TB, TBW = d["DC"], d["TOK"], d["TI"], d["TB"], d["TBW"]
    OB, OBW = d["OB"], d["OBW"]
    NQ, NKV, KC, GRP = d["NQ"], d["NKV"], d["KC"], d["GRP"]
    CPT, NCH, SA, SB = d["CPT"], d["NCH"], d["SA"], d["SB"]
    SCALE, KV_D, NRANK = d["SCALE"], d["KV_D"], d["NRANK"]

    nc = bass.Bass("TRN2", target_bir_lowering=False, debug=False)

    x_in = nc.dram_tensor("x_loc", [TOK, D], F32, kind="ExternalInput")
    cos_in = nc.dram_tensor("rope_cos", [128, TOK], F32, kind="ExternalInput")
    sin_in = nc.dram_tensor("rope_sin", [128, TOK], F32, kind="ExternalInput")
    mask_in = nc.dram_tensor("masks", [SB, 128, C], BF16, kind="ExternalInput")
    wq_in = nc.dram_tensor("wq_s", [NQ, 128, D], F32, kind="ExternalInput")
    wk_in = nc.dram_tensor("wk_s", [KC, 128, D], F32, kind="ExternalInput")
    wv_in = nc.dram_tensor("wv_s", [KC, 128, D], F32, kind="ExternalInput")
    wo_in = nc.dram_tensor("wo_s", [OB, 128, DC * OBW], F32, kind="ExternalInput")
    bw_in = nc.dram_tensor("bw_s", [OB, 128, DC * OBW], F32, kind="ExternalInput")
    lg_in = nc.dram_tensor("lg_s", [128, DC * 136], F32, kind="ExternalInput")
    bfl_in = nc.dram_tensor("bfl_s", [128, D], F32, kind="ExternalInput")
    bias_in = nc.dram_tensor("bias_s", [1, D], F32, kind="ExternalInput")
    cst_in = nc.dram_tensor("consts", [128, 520], F32, kind="ExternalInput")
    sml_in = nc.dram_tensor("smalls", [128, 2], F32, kind="ExternalInput")
    y_out = nc.dram_tensor("y_loc", [TOK, D], F32, kind="ExternalOutput")

    x1d = nc.dram_tensor("x1d", [TOK, D], F32)
    x2d = nc.dram_tensor("x2d", [TOK, D], F32)
    qrd = nc.dram_tensor("qrd", [NQ, 128, TOK], BF16)
    KVR = 2 * KV_D + 1
    kv_in_d = nc.dram_tensor("kv_cc_in", [KVR, TOK], BF16)
    kv_out_d = nc.dram_tensor("kv_cc_out", [NRANK * KVR, TOK], BF16)

    A0 = 0x04400
    B0 = 0x14400
    C0 = 0x24400
    ST = 0x2A400

    def at(name, shape, dtype, off):
        return nc.alloc_sbuf_tensor_at(name, shape, dtype, offset=off)

    lnT = at("lnT", [128, DC * TOK], BF16, A0)
    ctxT = at("ctxT", [128, NQ * TOK], BF16, A0)
    wpanA = at("wpanA", [128, DC * OBW], BF16, B0)
    wpanB = at("wpanB", [128, DC * OBW], BF16, B0 + DC * OBW * 2)
    kpan = [at("kpan0", [128, D], BF16, B0),
            at("kpan1", [128, D], BF16, B0 + D * 2)]
    ksb = at("ksb", [128, T], BF16, B0)
    vtsb = at("vtsb", [128, T], BF16, B0 + T * 2)
    vsb = at("vsb", [128, T], BF16, B0 + 2 * T * 2)
    cossb = at("cossb", [128, TOK], F32, B0 + 3 * T * 2)
    sinsb = at("sinsb", [128, TOK], F32, B0 + 3 * T * 2 + TOK * 4)
    msksb = at("msksb", [128, SB * C], BF16, B0 + 3 * T * 2 + 2 * TOK * 4)
    assert 3 * T * 2 + 2 * TOK * 4 + SB * C * 2 <= 0x10000, "region B overflow"
    xtile = at("xtile", [128, D], F32, C0)
    ntile = at("ntile", [128, D], BF16, C0 + D * 4)
    qsb = [at(f"qsb{i}", [128, TOK], BF16, C0 + i * TOK * 2)
           for i in range(4)]
    PW = 512 + C
    psb = [at(f"psb{i}", [128, PW], BF16, C0 + 4 * TOK * 2 + i * PW * 2)
           for i in range(3)]
    rsb = [at("rsb0", [128, PW], F32, C0 + 4 * TOK * 2 + 3 * PW * 2)]

    cur = [ST]

    def st_alloc(name, shape, dtype):
        nbytes = int(np.prod(shape[1:])) * (2 if dtype == BF16 else 4)
        nbytes = (nbytes + 31) // 32 * 32
        t = at(name, shape, dtype, cur[0])
        cur[0] += nbytes
        return t

    lgsb = st_alloc("lgsb", [128, DC * 136], BF16)
    bflsb = st_alloc("bflsb", [128, D], BF16)
    biasb = st_alloc("biasb", [1, D], BF16)
    slowT = st_alloc("slowT", [128, TOK], BF16)
    wT_sb = st_alloc("wT_sb", [8, TOK], BF16)
    G_sb = st_alloc("G_sb", [128, TBW], BF16)
    stats = st_alloc("stats", [128, TI * 48], F32)
    stg = [st_alloc(f"stg{i}", [128, TBW], BF16) for i in range(4)]
    rtmp = [st_alloc(f"rtmp{i}", [128, TBW], BF16) for i in range(2)]
    sqtl = [st_alloc(f"sqt{i}", [128, TBW], BF16) for i in range(2)]
    xrt = [st_alloc(f"xrt{i}", [128, OBW], F32) for i in range(2)]
    idsb = st_alloc("idsb", [128, 128], BF16)
    e8sb = st_alloc("e8sb", [8, 128], BF16)
    onesr = st_alloc("onesr", [1, 128], BF16)
    onesm = st_alloc("onesm", [128, 128], BF16)
    mv = st_alloc("mv", [128, 2], F32)
    ab_sb = st_alloc("ab_sb", [128, 2], F32)
    gtile = st_alloc("gtile", [128, 8], F32)
    gm_t = st_alloc("gm_t", [128, 8], F32)
    is1_t = st_alloc("is1_t", [128, 8], F32)
    is2_t = st_alloc("is2_t", [128, 8], F32)
    v1_t = st_alloc("v1_t", [128, 1], F32)
    v2_t = st_alloc("v2_t", [128, 1], F32)
    w1_t = st_alloc("w1_t", [128, 1], F32)
    w2_t = st_alloc("w2_t", [128, 1], F32)
    wf_t = st_alloc("wf_t", [128, 8], BF16)
    mqr = st_alloc("mqr", [1, 2 * NQ], F32)
    mkr = st_alloc("mkr", [1, 2 * NKV], F32)
    mqx = st_alloc("mqx", [1, NQ], F32)
    mkx = st_alloc("mkx", [1, NKV], F32)
    mkxb = st_alloc("mkxb", [1, NKV], BF16)
    mkg = st_alloc("mkg", [1, NRANK * NKV], BF16)
    zrow = st_alloc("zrow", [1, TOK], BF16)
    mke = st_alloc("mke", [1, NQ], F32)
    m2r = st_alloc("m2r", [1, NQ], F32)
    negm = st_alloc("negm", [1, NQ], F32)
    negmb = st_alloc("negmb", [1, NQ], BF16)
    smlf = st_alloc("smlf", [128, 2], F32)
    smlb = st_alloc("smlb", [128, 2], BF16)
    biasq = st_alloc("biasq", [128, NQ], F32)
    negp = st_alloc("negp", [1, NQ // 2], BF16)
    ksb2 = at("ksb2", [128, T], BF16, B0 + 3 * T * 2)
    vsb2 = at("vsb2", [128, T], BF16, B0 + T * 2)


    biasq2 = st_alloc("biasq2", [128, NQ // 2], F32)
    assert cur[0] <= 0x37F00, f"sbuf statics overflow: {cur[0]:#x}"

    PS = [nc.alloc_psum_tensor(f"PS{i}", [128, 1024], F32) for i in range(4)]

    class BV:
        def __init__(self, t, off):
            self.t, self.off = t, off

        def __getitem__(self, idx):
            r, c = idx
            return self.t[r, (c.start or 0) + self.off: c.stop + self.off]

    pb = [BV(PS[i // 2], (i % 2) * 512) for i in range(8)]
    Q1OFF = 512
    one_exp = (C == 512)

    stack = contextlib.ExitStack()
    S = Sched(nc)
    for s in ("pe", "ve", "ac", "dcn", "dwa", "dwb", "dwa0", "dwa1",
              "dwb0", "dwb1", "dmx", "dmg", "dxt", "dxt1", "dsg2", "dsg3",
              "dxr0", "dxr1", "dkv", "dms", "dq0", "dq1", "dsg0", "dsg1",
              "dcs", "dkv1", "dkvv", "dkvv1", "gp", "cc1", "cc2"):
        S.new_sem(stack, s)

    MM = lambda out, l, r, st_, sp: (
        lambda h: h.matmul(out, l, r, start=st_, stop=sp))

    toks = [
        S.dma("gpsimd", lgsb[:, :], lg_in[:, :], "dcn"),
        S.dma("gpsimd", bflsb[:, :], bfl_in[:, :], "dcn"),
        S.dma("gpsimd", biasb[:, :], bias_in[:, :], "dcn"),
        S.dma("gpsimd", idsb[:, :], cst_in[:, 0:128], "dcn"),
        S.dma("gpsimd", e8sb[:, :], cst_in[0:8, 128:256], "dcn"),
        S.dma("gpsimd", smlb[:, :], sml_in[:, :], "dcn"),
        S.dma("gpsimd", smlf[:, :], sml_in[:, :], "dcn"),
        S.dma("gpsimd", onesr[:, :], cst_in[0:1, 258:386], "dcn"),
        S.dma("gpsimd", onesm[:, :], cst_in[:, 386:514], "dcn"),
    ]
    t_consts = tmax(*toks)

    def emit_layernorm(src_dram, fused_stats, src_tok_list):
        ev_all = []
        prev_norm = None
        prev_rstd = None
        tile_toks = []
        for ti in range(TI):
            halves = []
            w0 = tmax(src_tok_list[ti] if src_tok_list else None, prev_norm)
            for hh in range(2):
                halves.append(S.dma(
                    "sync", xtile[:, hh * (D // 2):(hh + 1) * (D // 2)],
                    src_dram[ti * 128:(ti + 1) * 128, hh * (D // 2):(hh + 1) * (D // 2)],
                    ("dxt", "dxt1")[hh], waits=w0))
            if fused_stats is None:
                bt = None
                for p in range(D // 512):
                    if (p + 1) * 512 <= D // 2:
                        hw = halves[0]
                    elif p * 512 >= D // 2:
                        hw = halves[1]
                    else:
                        hw = tmax(halves[0], halves[1])
                    bt = S.emit("vector",
                        lambda h, ti=ti, p=p: h.bn_stats(
                            stats[:, ti * 48 + p * 6: ti * 48 + p * 6 + 6],
                            xtile[:, p * 512:(p + 1) * 512]),
                        waits=tmax(hw, prev_rstd),
                        inc="ve")
                t_agg = S.emit("vector",
                    lambda h, ti=ti: h.bn_aggr(
                        mv[:, :], stats[:, ti * 48: ti * 48 + 6 * (D // 512)]),
                    waits=tmax(bt), inc="ve")
            else:
                t_agg = S.emit("vector",
                    lambda h, ti=ti: h.bn_aggr(
                        mv[:, :], stats[:, ti * 48: ti * 48 + 6 * (D // 512)]),
                    waits=tmax(fused_stats[ti], prev_rstd, halves[1]), inc="ve")
            t_ln = S.emit("scalar",
                lambda h: h.activation(ab_sb[:, 0:1], mv[:, 1:2], AF.Ln,
                                       bias=smlf[:, 1:2]),
                waits=tmax(t_agg, t_consts), inc="ac")
            t_a = S.emit("scalar",
                lambda h: h.activation(ab_sb[:, 0:1], ab_sb[:, 0:1], AF.Exp,
                                       scale=-0.5),
                waits=tmax(t_ln), inc="ac")
            prev_rstd = t_a
            t_b = S.emit("vector",
                lambda h: h.scalar_tensor_tensor(ab_sb[:, 1:2], mv[:, 0:1], -1.0,
                                                 ab_sb[:, 0:1], ALU.mult, ALU.mult),
                waits=tmax(t_a), inc="ve")
            t_n = S.emit("gpsimd",
                lambda h: h.tensor_scalar(ntile[:, :], xtile[:, :],
                                          ab_sb[:, 0:1], ab_sb[:, 1:2],
                                          ALU.mult, ALU.add),
                waits=tmax(t_b, halves[1]), inc="gp")
            prev_norm = t_n
            tp = []
            for dc in range(DC):
                bank = pb[4 + (dc % 2)]
                wts = [t_n, t_consts]
                if len(tp) >= 2:
                    wts.append(tp[-2])
                t_mm = S.emit("tensor",
                    MM(bank[:, 0:128], ntile[:, dc * 128:(dc + 1) * 128],
                       idsb[:, :], True, True),
                    waits=tmax(*wts), inc="pe")
                if dc % 2 == 0:
                    t_ev = S.emit("vector",
                        lambda h, dc=dc, ti=ti, bank=bank: h.tensor_copy(
                            lnT[:, dc * TOK + ti * 128: dc * TOK + ti * 128 + 128],
                            bank[:, 0:128]),
                        waits=tmax(t_mm), inc="ve")
                else:
                    t_ev = S.emit("scalar",
                        lambda h, dc=dc, ti=ti, bank=bank: h.activation(
                            lnT[:, dc * TOK + ti * 128: dc * TOK + ti * 128 + 128],
                            bank[:, 0:128], AF.Copy),
                        waits=tmax(t_mm), inc="ac")
                tp.append(t_ev)
            ev_all.append(tmax(*tp))
            tile_toks.append(tmax(*tp))
        return tmax(*ev_all), tile_toks

    def emit_ffn(lnT_tok, ln_tiles, res_dram, out_dram, stats_out, pan_init,
                 res_wait):
        low_free = None
        g_free = None
        G_free = None
        slow_done = []
        for tb in range(TB):
            for dc in range(DC):
                t_low = S.emit("tensor",
                    MM(pb[2][:, 0:TBW], lgsb[:, dc * 136: dc * 136 + 128],
                       lnT[:, dc * TOK + tb * TBW: dc * TOK + tb * TBW + TBW],
                       dc == 0, dc == DC - 1),
                    waits=tmax(*(ln_tiles[tb * (TBW // 128):(tb + 1) * (TBW // 128)]),
                               t_consts, low_free) if dc == 0 else (),
                    inc="pe" if dc == DC - 1 else None)
            for tt in range(TBW // 128):
                ti = tb * (TBW // 128) + tt
                for dc in range(DC):
                    t_gmm = S.emit("tensor",
                        MM(pb[3][:, 0:8],
                           lnT[:, dc * TOK + ti * 128: dc * TOK + ti * 128 + 128],
                           lgsb[:, dc * 136 + 128: dc * 136 + 136],
                           dc == 0, dc == DC - 1),
                        waits=tmax(g_free, ln_tiles[ti]) if dc == 0 else (),
                        inc="pe" if dc == DC - 1 else None)
                t_g = S.emit("vector",
                    lambda h: h.tensor_copy(gtile[:, :], pb[3][:, 0:8]),
                    waits=tmax(t_gmm), inc="ve")
                t_v1 = S.emit("vector",
                    lambda h: h.reduce_max(v1_t[:, :], gtile[:, :], axis=AX.X),
                    waits=tmax(t_g), inc="ve")
                t_i1 = S.emit("vector",
                    lambda h: h.tensor_scalar(is1_t[:, :], gtile[:, :],
                                              v1_t[:, :], None, ALU.is_ge),
                    waits=tmax(t_v1), inc="ve")
                t_gm = S.emit("vector",
                    lambda h: h.scalar_tensor_tensor(gm_t[:, :], is1_t[:, :],
                                                     -1e30, gtile[:, :],
                                                     ALU.mult, ALU.add),
                    waits=tmax(t_i1), inc="ve")
                t_v2 = S.emit("vector",
                    lambda h: h.reduce_max(v2_t[:, :], gm_t[:, :], axis=AX.X),
                    waits=tmax(t_gm), inc="ve")
                t_i2 = S.emit("vector",
                    lambda h: h.tensor_scalar(is2_t[:, :], gm_t[:, :],
                                              v2_t[:, :], None, ALU.is_ge),
                    waits=tmax(t_v2), inc="ve")
                t_dd = S.emit("vector",
                    lambda h: h.tensor_sub(v2_t[:, :], v2_t[:, :], v1_t[:, :]),
                    waits=tmax(t_i2), inc="ve")
                t_e = S.emit("scalar",
                    lambda h: h.activation(w1_t[:, :], v2_t[:, :], AF.Exp),
                    waits=tmax(t_dd), inc="ac")
                t_p1 = S.emit("vector",
                    lambda h: h.tensor_scalar(w1_t[:, :], w1_t[:, :], 1.0,
                                              None, ALU.add),
                    waits=tmax(t_e), inc="ve")
                t_r = S.emit("vector",
                    lambda h: h.reciprocal(w1_t[:, :], w1_t[:, :]),
                    waits=tmax(t_p1), inc="ve")
                t_w2 = S.emit("vector",
                    lambda h: h.tensor_scalar(w2_t[:, :], w1_t[:, :], -1.0, 1.0,
                                              ALU.mult, ALU.add),
                    waits=tmax(t_r), inc="ve")
                t_m1 = S.emit("vector",
                    lambda h: h.tensor_scalar(is1_t[:, :], is1_t[:, :],
                                              w1_t[:, :], None, ALU.mult),
                    waits=tmax(t_r), inc="ve")
                t_m2 = S.emit("vector",
                    lambda h: h.tensor_scalar(is2_t[:, :], is2_t[:, :],
                                              w2_t[:, :], None, ALU.mult),
                    waits=tmax(t_w2), inc="ve")
                t_wf = S.emit("vector",
                    lambda h: h.tensor_add(wf_t[:, :], is1_t[:, :], is2_t[:, :]),
                    waits=tmax(t_m1, t_m2), inc="ve")
                t_wmm = S.emit("tensor",
                    MM(pb[3][0:8, 128:256], wf_t[:, :], idsb[:, :], True, True),
                    waits=tmax(t_wf, t_g), inc="pe")
                t_wev = S.emit("vector",
                    lambda h, ti=ti: h.tensor_copy(
                        wT_sb[0:8, ti * 128: ti * 128 + 128], pb[3][0:8, 128:256]),
                    waits=tmax(t_wmm), inc="ve")
                g_free = t_wev
            t_G = S.emit("tensor",
                MM(pb[6][:, 0:TBW], e8sb[:, :],
                   wT_sb[0:8, tb * TBW:(tb + 1) * TBW], True, True),
                waits=tmax(g_free, t_consts, G_free), inc="pe")
            t_Ge = S.emit("vector",
                lambda h: h.tensor_copy(G_sb[:, 0:TBW], pb[6][:, 0:TBW]),
                waits=tmax(t_G), inc="ve")
            G_free = t_Ge
            t_sl = S.emit("vector",
                lambda h, tb=tb: h.tensor_mul(slowT[:, tb * TBW:(tb + 1) * TBW],
                                              pb[2][:, 0:TBW], G_sb[:, 0:TBW]),
                waits=tmax(t_Ge, t_low), inc="ve")
            low_free = t_sl
            slow_done.append(t_sl)
        t_slow = tmax(*slow_done)

        pan_last = list(pan_init)
        ev_toks, stat_toks = [], []
        xr_free = [None, None]
        HW = DC // 2 * OBW
        for ob in range(OB):
            pan = (wpanA, wpanB)[ob % 2]
            pan_tok = [
                S.dma("gpsimd", pan[:, 0:HW], bw_in[ob, :, 0:HW],
                      ("dwa0", "dwb0")[ob % 2], waits=tmax(pan_last[ob % 2])),
                S.dma("gpsimd", pan[:, HW:2 * HW], bw_in[ob, :, HW:2 * HW],
                      ("dwa1", "dwb1")[ob % 2], waits=tmax(pan_last[ob % 2])),
            ]
            for ti in range(TI):
                bank = pb[ti % 2]
                wts = tmax(pan_tok[0], ln_tiles[ti], t_slow, t_consts,
                           ev_toks[-2] if len(ev_toks) >= 2 else None)
                for dc in range(DC):
                    S.emit("tensor",
                        MM(bank[:, 0:OBW],
                           lnT[:, dc * TOK + ti * 128: dc * TOK + ti * 128 + 128],
                           pan[:, dc * OBW:(dc + 1) * OBW], dc == 0, False),
                        waits=wts if dc == 0 else
                        (tmax(pan_tok[1]) if dc == DC // 2 else ()))
                S.emit("tensor",
                    MM(bank[:, 0:OBW], onesr[:, :],
                       biasb[0:1, ob * OBW:(ob + 1) * OBW], False, False))
                t_mm = S.emit("tensor",
                    MM(bank[:, 0:OBW], slowT[:, ti * 128:(ti + 1) * 128],
                       bflsb[:, ob * OBW:(ob + 1) * OBW], False, True),
                    inc="pe")
                pan_last[ob % 2] = t_mm
                u = (ob * TI + ti) % 2
                xr = xrt[u]
                t_xr = S.dma("sync", xr[:, :],
                             res_dram[ti * 128:(ti + 1) * 128,
                                      ob * OBW:(ob + 1) * OBW],
                             ("dxr0", "dxr1")[u], waits=tmax(xr_free[u], res_wait))
                t_ev = S.emit("vector",
                    lambda h, bank=bank, xr=xr: h.scalar_tensor_tensor(
                        xr[:, :], bank[:, 0:OBW], 0.5, xr[:, :],
                        ALU.mult, ALU.add),
                    waits=tmax(t_mm, t_xr), inc="ve")
                t_after = t_ev
                if stats_out:
                    t_after = S.emit("vector",
                        lambda h, ti=ti, ob=ob, xr=xr: h.bn_stats(
                            stats[:, ti * 48 + ob * 6: ti * 48 + ob * 6 + 6],
                            xr[:, :]),
                        waits=tmax(t_ev), inc="ve")
                    stat_toks.append(t_after)
                t_st = S.dma("sync",
                             out_dram[ti * 128:(ti + 1) * 128,
                                      ob * OBW:(ob + 1) * OBW],
                             xr[:, :], ("dxr0", "dxr1")[u], waits=tmax(t_after))
                xr_free[u] = t_st
                ev_toks.append(tmax(t_after, t_st))
        last_ti = [tmax(ev_toks[(OB - 1) * TI + t_]) for t_ in range(TI)]
        return (tmax(*ev_toks),
                last_ti if stats_out else None,
                pan_last)

    t_ln1, ln1_tiles = emit_layernorm(x_in, None, None)
    t_ffn1, t_st1, pan_last1 = emit_ffn(t_ln1, ln1_tiles, x_in, x1d, True,
                                        [None, None], None)

    t_ln2, ln2_tiles = emit_layernorm(x1d, t_st1, t_st1)

    t_cos = S.dma("gpsimd", cossb[:, :], cos_in[:, :], "dcs",
                  waits=tmax(pan_last1[0], pan_last1[1]))
    t_sin = S.dma("gpsimd", sinsb[:, :], sin_in[:, :], "dcs",
                  waits=tmax(pan_last1[0], pan_last1[1]))
    mtok = []
    for s_ in range(SB):
        mtok.append(S.dma("sync", msksb[:, s_ * C:(s_ + 1) * C],
                          mask_in[s_, :, :], "dms",
                          waits=tmax(pan_last1[0], pan_last1[1])))
    t_msk = tmax(*mtok)

    pan_busy = list(pan_last1)
    norm_pending = []

    def flush_norm():
        while norm_pending:
            norm_pending.pop(0)()

    bank_rd = [None] * 4
    nrm_rd = [None, None]
    stg_busy = [None] * 4
    SGS = ("dsg0", "dsg1", "dsg2", "dsg3")
    kv_store = []
    mk_red = []
    uc = 0
    for which, w_in, rope in (("k", wk_in, True), ("v", wv_in, False)):
        for kf in range(KC):
            pan = kpan[kf % 2]
            t_pan = S.dma("gpsimd", pan[:, :], w_in[kf, :, :],
                          ("dwa", "dwb")[kf % 2], waits=tmax(pan_busy[kf % 2]))
            for tb in range(TB):
                bank = pb[uc % 4]
                wts = tmax(t_pan, bank_rd[uc % 4], *ln2_tiles[tb * (TBW // 128):(tb + 1) * (TBW // 128)])
                for dc in range(DC):
                    t_mm = S.emit("tensor",
                        MM(bank[:, 0:TBW], pan[:, dc * 128:(dc + 1) * 128],
                           lnT[:, dc * TOK + tb * TBW: dc * TOK + tb * TBW + TBW],
                           dc == 0, dc == DC - 1),
                        waits=wts if dc == 0 else (),
                        inc="pe" if dc == DC - 1 else None)
                pan_busy[kf % 2] = t_mm
                sg = stg[uc % 4]
                if rope:
                    t_c = S.emit("vector",
                        lambda h, bank=bank, tb=tb: h.tensor_mul(
                            rtmp[0][:, :], bank[:, 0:TBW],
                            cossb[:, tb * TBW:(tb + 1) * TBW]),
                        waits=tmax(t_mm, t_cos, t_sin), inc="ve")
                    t_s1 = S.emit("vector",
                        lambda h, bank=bank, tb=tb: h.tensor_mul(
                            rtmp[1][0:64, :], bank[64:128, 0:TBW],
                            sinsb[0:64, tb * TBW:(tb + 1) * TBW]),
                        inc="ve")
                    t_s2 = S.emit("vector",
                        lambda h, bank=bank, tb=tb: h.tensor_mul(
                            rtmp[1][64:128, :], bank[0:64, 0:TBW],
                            sinsb[64:128, tb * TBW:(tb + 1) * TBW]),
                        inc="ve")
                    t_kr = S.emit("vector",
                        lambda h, sg=sg: h.tensor_add(sg[:, :], rtmp[0][:, :],
                                                      rtmp[1][:, :]),
                        waits=tmax(t_c, t_s1, t_s2, stg_busy[uc % 4]), inc="ve")
                    sq = sqtl[uc % 2]
                    t_q2 = S.emit("vector",
                        lambda h, sg=sg, sq=sq: h.tensor_mul(sq[:, :], sg[:, :],
                                                             sg[:, :]),
                        waits=tmax(t_kr), inc="ve")
                    bank_rd[uc % 4] = t_q2
                    flush_norm()

                    def mk_norm(uc=uc, sq=sq, t_q2=t_q2, kf=kf, tb=tb):
                        nb = pb[6 + (uc % 2)]
                        t_nm = S.emit("tensor",
                            MM(nb[0:1, 0:TBW], smlb[:, 0:1], sq[:, :], True, True),
                            waits=tmax(t_q2, t_consts, nrm_rd[uc % 2]), inc="pe")
                        t_rd = S.emit("vector",
                            lambda h, nb=nb, kf=kf, tb=tb: h.reduce_max(
                                mkr[0:1, kf * TB + tb: kf * TB + tb + 1],
                                nb[0:1, 0:TBW], axis=AX.X),
                            waits=tmax(t_nm), inc="ve")
                        nrm_rd[uc % 2] = t_rd
                        mk_red.append(t_rd)
                    norm_pending.append(mk_norm)
                    t_done = t_q2
                else:
                    t_kr = S.emit("vector",
                        lambda h, bank=bank, sg=sg: h.tensor_copy(
                            sg[:, :], bank[:, 0:TBW]),
                        waits=tmax(t_mm, stg_busy[uc % 4]), inc="ve")
                    bank_rd[uc % 4] = t_kr
                    t_done = t_kr
                row0 = (0 if which == "k" else KV_D) + kf * 128
                t_st = S.dma("sync",
                             kv_in_d[row0:row0 + 128, tb * TBW:(tb + 1) * TBW],
                             sg[:, :], SGS[uc % 4],
                             waits=tmax(t_done))
                stg_busy[uc % 4] = t_st
                kv_store.append(t_st)
                uc += 1

    flush_norm()
    if TB == 2:
        t_mk = S.emit("vector",
            lambda h: h.tensor_max(mkx[0:1, 0:NKV], mkr[0:1, 0:2 * NKV:2],
                                   mkr[0:1, 1:2 * NKV:2]),
            waits=tmax(*mk_red), inc="ve")
    else:
        t_mk = S.emit("vector",
            lambda h: h.tensor_copy(mkx[0:1, 0:NKV], mkr[0:1, 0:NKV]),
            waits=tmax(*mk_red), inc="ve")
    t_mkb16 = S.emit("vector",
        lambda h: h.tensor_scalar(mkxb[0:1, 0:NKV], mkx[0:1, 0:NKV], 1.0,
                                  None, ALU.mult),
        waits=tmax(t_mk), inc="ve")
    t_zr = S.emit("vector", lambda h: h.memset(zrow[0:1, :], 0.0), inc="ve")
    t_zrd = S.dma("gpsimd", kv_in_d[2 * KV_D:2 * KV_D + 1, 0:TOK],
                  zrow[0:1, :], "dmx", waits=tmax(t_zr))
    t_mkd = S.dma("gpsimd", kv_in_d[2 * KV_D:2 * KV_D + 1, 0:NKV],
                  mkxb[0:1, 0:NKV], "dmx", waits=tmax(t_mkb16, t_zrd))

    cc_toks = {}

    def emit_collectives():
        cc_toks["cc1"] = S.emit("gpsimd",
            lambda h: h.collective_compute(
                "AllGather", ALU.bypass,
                replica_groups=[[0, 1, 2, 3], [4, 5, 6, 7]],
                ins=[kv_in_d.ap()], outs=[kv_out_d.ap()]),
            waits=tmax(*kv_store, t_mkd), inc="cc1")

    mq_red = []
    for qc in range(NQ):
        if qc == 2:
            emit_collectives()
        pan = kpan[qc % 2]
        t_pan = S.dma("gpsimd", pan[:, :], wq_in[qc, :, :],
                      ("dwa", "dwb")[qc % 2], waits=tmax(pan_busy[qc % 2]))
        for tb in range(TB):
            bank = pb[uc % 4]
            wts = tmax(t_pan, bank_rd[uc % 4], *ln2_tiles[tb * (TBW // 128):(tb + 1) * (TBW // 128)])
            for dc in range(DC):
                t_mm = S.emit("tensor",
                    MM(bank[:, 0:TBW], pan[:, dc * 128:(dc + 1) * 128],
                       lnT[:, dc * TOK + tb * TBW: dc * TOK + tb * TBW + TBW],
                       dc == 0, dc == DC - 1),
                    waits=wts if dc == 0 else (),
                    inc="pe" if dc == DC - 1 else None)
            pan_busy[qc % 2] = t_mm
            sg = stg[uc % 4]
            t_c = S.emit("vector",
                lambda h, bank=bank, tb=tb: h.tensor_mul(
                    rtmp[0][:, :], bank[:, 0:TBW],
                    cossb[:, tb * TBW:(tb + 1) * TBW]),
                waits=tmax(t_mm, t_cos, t_sin), inc="ve")
            t_s1 = S.emit("vector",
                lambda h, bank=bank, tb=tb: h.tensor_mul(
                    rtmp[1][0:64, :], bank[64:128, 0:TBW],
                    sinsb[0:64, tb * TBW:(tb + 1) * TBW]),
                inc="ve")
            t_s2 = S.emit("vector",
                lambda h, bank=bank, tb=tb: h.tensor_mul(
                    rtmp[1][64:128, :], bank[0:64, 0:TBW],
                    sinsb[64:128, tb * TBW:(tb + 1) * TBW]),
                inc="ve")
            t_kr = S.emit("vector",
                lambda h, sg=sg: h.tensor_add(sg[:, :], rtmp[0][:, :], rtmp[1][:, :]),
                waits=tmax(t_c, t_s1, t_s2, stg_busy[uc % 4]), inc="ve")
            sq = sqtl[uc % 2]
            t_q2 = S.emit("vector",
                lambda h, sg=sg, sq=sq: h.tensor_mul(sq[:, :], sg[:, :], sg[:, :]),
                waits=tmax(t_kr), inc="ve")
            bank_rd[uc % 4] = t_q2
            flush_norm()

            def mq_norm(uc=uc, sq=sq, t_q2=t_q2, qc=qc, tb=tb):
                nb = pb[6 + (uc % 2)]
                t_nm = S.emit("tensor",
                    MM(nb[0:1, 0:TBW], smlb[:, 0:1], sq[:, :], True, True),
                    waits=tmax(t_q2, t_consts, nrm_rd[uc % 2]), inc="pe")
                t_rd = S.emit("vector",
                    lambda h, nb=nb, qc=qc, tb=tb: h.reduce_max(
                        mqr[0:1, qc * TB + tb: qc * TB + tb + 1],
                        nb[0:1, 0:TBW], axis=AX.X),
                    waits=tmax(t_nm), inc="ve")
                nrm_rd[uc % 2] = t_rd
                mq_red.append(t_rd)
            norm_pending.append(mq_norm)
            t_st = S.dma("sync", qrd[qc, :, tb * TBW:(tb + 1) * TBW], sg[:, :],
                         SGS[uc % 4], waits=tmax(t_q2))
            stg_busy[uc % 4] = t_st
            uc += 1

    flush_norm()
    t_qst = [(s_, S.cnt[s_]) for s_ in SGS]
    if TB == 2:
        t_mq = S.emit("vector",
            lambda h: h.tensor_max(mqx[0:1, 0:NQ], mqr[0:1, 0:2 * NQ:2],
                                   mqr[0:1, 1:2 * NQ:2]),
            waits=tmax(*mq_red), inc="ve")
    else:
        t_mq = S.emit("vector",
            lambda h: h.tensor_copy(mqx[0:1, 0:NQ], mqr[0:1, 0:NQ]),
            waits=tmax(*mq_red), inc="ve")
    t_mg = S.dma("sync", mkg[0:1, 0:NKV],
                 kv_out_d[2 * KV_D:2 * KV_D + 1, 0:NKV],
                 "dmg", waits=tmax(cc_toks["cc1"]))
    mg_toks = [t_mg]
    for r_ in range(1, NRANK):
        mg_toks.append(S.dma(
            "sync", mkg[0:1, r_ * NKV:(r_ + 1) * NKV],
            kv_out_d[r_ * KVR + 2 * KV_D:r_ * KVR + 2 * KV_D + 1, 0:NKV],
            "dmg", waits=()))
    t_mx = None
    for r_ in range(1, NRANK):
        t_mx = S.emit("vector",
            lambda h, r_=r_: h.tensor_tensor(
                mkg[0:1, 0:NKV], mkg[0:1, 0:NKV],
                mkg[0:1, r_ * NKV:(r_ + 1) * NKV], ALU.max),
            waits=tmax(*mg_toks, t_mx), inc="ve")
    t_mkb = t_mx
    t_me = None
    for r in range(GRP):
        t_me = S.emit("vector",
            lambda h, r=r: h.tensor_copy(mke[0:1, r:NQ:GRP], mkg[0:1, 0:NKV]),
            waits=tmax(t_mkb, t_me), inc="ve")
    t_m2 = S.emit("vector",
        lambda h: h.tensor_mul(m2r[0:1, 0:NQ], mqx[0:1, 0:NQ], mke[0:1, 0:NQ]),
        waits=tmax(t_mq, t_me), inc="ve")
    t_l2 = S.emit("scalar",
        lambda h: h.activation(m2r[0:1, 0:NQ], m2r[0:1, 0:NQ], AF.Ln),
        waits=tmax(t_m2), inc="ac")
    t_ex = S.emit("scalar",
        lambda h: h.activation(negm[0:1, 0:NQ], m2r[0:1, 0:NQ], AF.Exp, scale=0.5),
        waits=tmax(t_l2), inc="ac")
    t_ng = S.emit("vector",
        lambda h: h.tensor_scalar(negmb[0:1, 0:NQ], negm[0:1, 0:NQ],
                                  -SCALE * 1.02, None, ALU.mult),
        waits=tmax(t_ex), inc="ve")
    t_bq = S.emit("tensor",
        MM(pb[5][:, 0:NQ], onesr[:, :], negmb[0:1, 0:NQ], True, True),
        waits=tmax(t_ng, t_consts), inc="pe")
    t_bqe = S.emit("vector",
        lambda h: h.tensor_copy(biasq[:, :], pb[5][:, 0:NQ]),
        waits=tmax(t_bq), inc="ve")
    t_np = S.emit("vector",
        lambda h: h.tensor_tensor(negp[0:1, 0:NQ // 2], negmb[0:1, 0:NQ:2],
                                  negmb[0:1, 1:NQ:2], ALU.min),
        waits=tmax(t_ng), inc="ve")
    t_bq2 = S.emit("tensor",
        MM(pb[4][:, 0:NQ // 2], onesr[:, :], negp[0:1, 0:NQ // 2], True, True),
        waits=tmax(t_np, t_bqe), inc="pe")
    t_bqe2 = S.emit("vector",
        lambda h: h.tensor_copy(biasq2[:, :], pb[4][:, 0:NQ // 2]),
        waits=tmax(t_bq2), inc="ve")
    t_bqe = tmax(t_bqe, t_bqe2)

    kv_ready = tmax(cc_toks["cc1"])
    ctx_ev = []
    q_busy = [tmax(t_ln2), tmax(t_ln2)]
    p_busy = [tmax(t_ln2), tmax(t_ln2)]
    r_busy = [tmax(t_ln2), tmax(t_ln2)]
    def q0_of(u_pair):
        return (u_pair // (GRP // 2)) * GRP + 2 * (u_pair % (GRP // 2))

    NPAIR = NKV * (GRP // 2)
    qtok = {}

    def emit_q_load(u_pair):
        if u_pair >= NPAIR or u_pair in qtok:
            return
        us_ = u_pair % 2
        t0_ = S.dma("scalar", qsb[2 * us_][:, :], qrd[q0_of(u_pair), :, :],
                    ("dq0", "dq1")[us_], waits=tmax(q_busy[us_], t_qst))
        t1_ = S.dma("scalar", qsb[2 * us_ + 1][:, :],
                    qrd[q0_of(u_pair) + 1, :, :],
                    ("dq0", "dq1")[us_], waits=tmax(q_busy[us_], t_qst))
        qtok[u_pair] = tmax(t0_, t1_)

    kvset = [(ksb, vsb, "dkv"), (ksb2, vsb2, "dkv1")]
    kv_busy = [None, None]
    kv_load = [None, None]

    def emit_kv_load(h_kv):
        kbuf, vbuf, sem = kvset[h_kv % 2]
        vsem = ("dkvv", "dkvv1")[h_kv % 2]
        kgate = tmax(kv_ready, kv_busy[h_kv % 2],
                     pan_busy[0] if h_kv % 2 == 0 else t_bqe)
        vgate = tmax(kv_ready, kv_busy[h_kv % 2],
                     None if h_kv % 2 == 0 else pan_busy[1])
        k_toks = []
        for ch in range(NCH):
            jo = min(ch, NCH - 1 - ch)
            side = 0 if ch == jo else 1
            srk = jo * KVR + h_kv * 128
            srv = jo * KVR + KV_D + h_kv * 128
            t1 = S.dma("sync", kbuf[:, ch * C:(ch + 1) * C],
                       kv_out_d[srk:srk + 128, side * C: side * C + C],
                       sem, waits=kgate)
            k_toks.append(t1)
            for tt in range(CPT):
                t2 = S.emit("sync",
                    lambda h, vbuf=vbuf, ch=ch, tt=tt, srv=srv, side=side:
                    h.dma_start_transpose(
                        vbuf[:, (ch * CPT + tt) * 128:(ch * CPT + tt) * 128 + 128],
                        kv_out_d[srv:srv + 128,
                                 side * C + tt * 128: side * C + tt * 128 + 128]),
                    waits=vgate, inc=vsem, incn=16)
                k_toks.append(t2)
        kv_load[h_kv % 2] = tmax(*k_toks)

    emit_kv_load(0)
    for h_kv in range(NKV):
        ksb_c, vsb_c, _ = kvset[h_kv % 2]
        t_kload = kv_load[h_kv % 2]
        vt_done = t_kload
        if h_kv + 1 < NKV:
            emit_kv_load(h_kv + 1)

        ksb = ksb_c
        vsb = vsb_c
        for pr in range(GRP // 2):
            u_pair = h_kv * (GRP // 2) + pr
            us = u_pair % 2
            qA, qB = qsb[2 * us], qsb[2 * us + 1]
            emit_q_load(u_pair)
            q0 = h_kv * GRP + 2 * pr
            q1 = q0 + 1
            t_q = tmax(qtok[u_pair], t_bqe)
            for cki, nslot in ((0, SA), (1, SB)):
                qoff = cki * C
                exp_toks = []
                pm_toks = []
                sc_toks = []

                def emit_scores(s):
                    SC = PS[s % 2]
                    wts = tmax(t_kload, t_q,
                               exp_toks[s - 2] if s >= 2 else None)
                    if s == 0 and not ctx_ev:
                        wts = tmax(wts, t_bqe)
                    S.emit("tensor",
                        MM(SC[:, 0:C], ksb[:, s * 128: s * 128 + 128],
                           qA[:, qoff: qoff + C], True, True),
                        waits=wts)
                    t = S.emit("tensor",
                        MM(SC[:, Q1OFF:Q1OFF + C], ksb[:, s * 128: s * 128 + 128],
                           qB[:, qoff: qoff + C], True, True),
                        inc="pe")
                    sc_toks.append(t)
                    pt = psb[s % 3]
                    pw = tmax(t, sc_toks[s - 2] if s >= 2 else None)
                    if one_exp:
                        t_e = S.emit("scalar",
                            lambda h, SC=SC, pt=pt: h.activation(
                                pt[:, 0:2 * C], SC[:, 0:2 * C], AF.Exp,
                                bias=biasq2[:, q0 // 2:q0 // 2 + 1], scale=SCALE),
                            waits=pw, inc="ac")
                    else:
                        S.emit("scalar",
                            lambda h, SC=SC, pt=pt: h.activation(
                                pt[:, 0:C], SC[:, 0:C], AF.Exp,
                                bias=biasq[:, q0:q0 + 1], scale=SCALE),
                            waits=pw, inc="ac")
                        t_e = S.emit("scalar",
                            lambda h, SC=SC, pt=pt: h.activation(
                                pt[:, Q1OFF:Q1OFF + C], SC[:, Q1OFF:Q1OFF + C],
                                AF.Exp, bias=biasq[:, q1:q1 + 1], scale=SCALE),
                            inc="ac")
                    exp_toks.append(t_e)
                    masked = (cki == 0) or (s >= SA)
                    if masked:
                        S.emit("vector",
                            lambda h, pt=pt, s=s: h.tensor_mul(
                                pt[:, 0:C], pt[:, 0:C],
                                msksb[:, s * C:(s + 1) * C]),
                            waits=tmax(t_e, t_msk), inc="ve")
                        t_p = S.emit("vector",
                            lambda h, pt=pt, s=s: h.tensor_mul(
                                pt[:, Q1OFF:Q1OFF + C], pt[:, Q1OFF:Q1OFF + C],
                                msksb[:, s * C:(s + 1) * C]),
                            inc="ve")
                        pm_toks.append(t_p)
                    else:
                        pm_toks.append(t_e)

                def emit_accum(s):
                    pt = psb[s % 3]
                    wts = tmax(pm_toks[s], t_consts)
                    if s == 0:
                        wts = tmax(wts, ctx_ev[-1] if ctx_ev else None, vt_done)
                    S.emit("tensor",
                        MM(PS[3][:, 0:C], onesm[:, :], pt[:, 0:C],
                           s == 0, s == nslot - 1),
                        waits=wts)
                    S.emit("tensor",
                        MM(PS[3][:, 512:512 + C], onesm[:, :],
                           pt[:, Q1OFF:Q1OFF + C], s == 0, s == nslot - 1))
                    S.emit("tensor",
                        MM(PS[2][:, 0:C], vsb[:, s * 128: s * 128 + 128],
                           pt[:, 0:C], s == 0, s == nslot - 1))
                    return S.emit("tensor",
                        MM(PS[2][:, 512:512 + C], vsb[:, s * 128: s * 128 + 128],
                           pt[:, Q1OFF:Q1OFF + C], s == 0, s == nslot - 1),
                        inc="pe")

                last_pe = None
                for s in range(nslot):
                    emit_scores(s)
                    if s >= 2:
                        last_pe = emit_accum(s - 2)
                last_pe = emit_accum(nslot - 2) if nslot >= 2 else last_pe
                last_pe = emit_accum(nslot - 1)

                if one_exp:
                    t_r0 = S.emit("vector",
                        lambda h: h.reciprocal(rsb[0][:, 0:512 + C],
                                               PS[3][:, 0:512 + C]),
                        waits=tmax(last_pe, r_busy[0], r_busy[1]), inc="ve")
                    t_r1 = t_r0
                else:
                    t_r0 = S.emit("vector",
                        lambda h: h.reciprocal(rsb[0][:, 0:C], PS[3][:, 0:C]),
                        waits=tmax(last_pe, r_busy[0], r_busy[1]), inc="ve")
                    t_r1 = S.emit("vector",
                        lambda h: h.reciprocal(rsb[0][:, 512:512 + C],
                                               PS[3][:, 512:512 + C]),
                        inc="ve")
                t_x0 = S.emit("vector",
                    lambda h, q0=q0, qoff=qoff: h.tensor_mul(
                        ctxT[:, q0 * TOK + qoff: q0 * TOK + qoff + C],
                        PS[2][:, 0:C], rsb[0][:, 0:C]),
                    waits=tmax(t_r0), inc="ve")
                t_x1 = S.emit("vector",
                    lambda h, q1=q1, qoff=qoff: h.tensor_mul(
                        ctxT[:, q1 * TOK + qoff: q1 * TOK + qoff + C],
                        PS[2][:, 512:512 + C], rsb[0][:, 512:512 + C]),
                    waits=tmax(t_r1), inc="ve")
                r_busy[0], r_busy[1] = t_x0, t_x1
                ctx_ev.append(tmax(t_x0, t_x1))
                if cki == 0:
                    emit_q_load(u_pair + 1)
            q_busy[us] = ctx_ev[-1]
        kv_busy[h_kv % 2] = ctx_ev[-1]
    t_ctx = tmax(*ctx_ev)

    ev3, st3 = [], []
    pan_last3 = [None, None]
    xr_free = [tmax(t_ctx), tmax(t_ctx)]
    HW3 = DC // 2 * OBW
    for ob in range(OB):
        pan = (wpanA, wpanB)[ob % 2]
        pan_tok3 = [
            S.dma("gpsimd", pan[:, 0:HW3], wo_in[ob, :, 0:HW3],
                  ("dwa0", "dwb0")[ob % 2],
                  waits=tmax(pan_last3[ob % 2], t_ctx)),
            S.dma("gpsimd", pan[:, HW3:2 * HW3], wo_in[ob, :, HW3:2 * HW3],
                  ("dwa1", "dwb1")[ob % 2],
                  waits=tmax(pan_last3[ob % 2], t_ctx)),
        ]
        for ti in range(TI):
            bank = pb[ti % 2]
            wts = tmax(pan_tok3[0], t_ctx, ev3[-2] if len(ev3) >= 2 else None)
            for dc in range(DC):
                t_mm = S.emit("tensor",
                    MM(bank[:, 0:OBW],
                       ctxT[:, dc * TOK + ti * 128: dc * TOK + ti * 128 + 128],
                       pan[:, dc * OBW:(dc + 1) * OBW], dc == 0, dc == DC - 1),
                    waits=wts if dc == 0 else
                    (tmax(pan_tok3[1]) if dc == DC // 2 else ()),
                    inc="pe" if dc == DC - 1 else None)
            pan_last3[ob % 2] = t_mm
            u = (ob * TI + ti) % 2
            xr = xrt[u]
            t_xr = S.dma("sync", xr[:, :],
                         x1d[ti * 128:(ti + 1) * 128, ob * OBW:(ob + 1) * OBW],
                         ("dxr0", "dxr1")[u], waits=tmax(xr_free[u]))
            t_ev = S.emit("vector",
                lambda h, bank=bank, xr=xr: h.tensor_add(
                    xr[:, :], bank[:, 0:OBW], xr[:, :]),
                waits=tmax(t_mm, t_xr), inc="ve")
            t_bs = S.emit("vector",
                lambda h, ti=ti, ob=ob, xr=xr: h.bn_stats(
                    stats[:, ti * 48 + ob * 6: ti * 48 + ob * 6 + 6], xr[:, :]),
                waits=tmax(t_ev), inc="ve")
            st3.append(t_bs)
            t_st = S.dma("sync",
                         x2d[ti * 128:(ti + 1) * 128, ob * OBW:(ob + 1) * OBW],
                         xr[:, :], ("dxr0", "dxr1")[u], waits=tmax(t_bs))
            xr_free[u] = t_st
            ev3.append(tmax(t_bs, t_st))
    t_x2 = tmax(*ev3)
    x2_tiles = [tmax(ev3[(OB - 1) * TI + t_]) for t_ in range(TI)]

    t_ln3, ln3_tiles = emit_layernorm(x2d, x2_tiles, x2_tiles)
    t_y, _, _ = emit_ffn(t_ln3, ln3_tiles, x2d, y_out, False, pan_last3, None)

    S.emit("sync", lambda h: h.nop(),
           waits=tmax(t_y, ("dxr0", S.cnt["dxr0"]), ("dxr1", S.cnt["dxr1"])))

    with nc.Block() as block:
        @block.tensor
        def _(h):
            S.replay("tensor", h)

        @block.vector
        def _(h):
            S.replay("vector", h)

        @block.scalar
        def _(h):
            S.replay("scalar", h)

        @block.gpsimd
        def _(h):
            S.replay("gpsimd", h)

        @block.sync
        def _(h):
            S.replay("sync", h)

    stack.close()
    return nc



def stage_weights(cfg, Wq, Wk, Wv, Wo, gate_w, base_w, base_b, lora_A, lora_B):
    d = derived(cfg)
    D, DC, OB, OBW = cfg["D"], d["DC"], d["OB"], d["OBW"]
    NQ, KC = cfg["NQ"], d["KC"]
    f32 = np.float32
    wq_s = np.ascontiguousarray(
        Wq.reshape(DC, 128, NQ, 128).transpose(2, 1, 0, 3).reshape(NQ, 128, D), f32)
    wk_s = np.ascontiguousarray(
        Wk.reshape(DC, 128, KC, 128).transpose(2, 1, 0, 3).reshape(KC, 128, D), f32)
    wv_s = np.ascontiguousarray(
        Wv.reshape(DC, 128, KC, 128).transpose(2, 1, 0, 3).reshape(KC, 128, D), f32)
    wo_s = np.ascontiguousarray(
        Wo.reshape(DC, 128, OB, OBW).transpose(2, 1, 0, 3).reshape(OB, 128, DC * OBW), f32)
    bw_s = np.ascontiguousarray(
        base_w.T.reshape(DC, 128, OB, OBW).transpose(2, 1, 0, 3)
        .reshape(OB, 128, DC * OBW), f32)
    A_flat = lora_A.reshape(128, D)
    lg_s = np.ascontiguousarray(
        np.concatenate([A_flat.T.reshape(DC, 128, 128),
                        gate_w.T.reshape(DC, 128, 8)], axis=2)
        .transpose(1, 0, 2).reshape(128, DC * 136), f32)
    bfl_s = np.ascontiguousarray(lora_B.transpose(0, 2, 1).reshape(128, D), f32)
    bias_s = np.ascontiguousarray(base_b.reshape(1, D), f32)
    consts = np.zeros((128, 520), f32)
    consts[:, 0:128] = np.eye(128, dtype=f32)
    for e in range(8):
        consts[e, 128 + e * 16:128 + (e + 1) * 16] = float(cfg["SCALING"])
    consts[0, 258:386] = 1.0
    consts[:, 386:514] = 1.0
    smalls = np.zeros((128, 2), f32)
    smalls[:, 0] = 1.0
    smalls[:, 1] = float(cfg["EPS"])
    return dict(wq_s=wq_s, wk_s=wk_s, wv_s=wv_s, wo_s=wo_s, bw_s=bw_s,
                lg_s=lg_s, bfl_s=bfl_s, bias_s=bias_s, consts=consts,
                smalls=smalls)


def rope_tables(positions):
    ROPE = dict(rope_theta=500000.0, factor=32.0, hi_freq_factor=4.0,
                lo_freq_factor=1.0, original_context_length=8192)
    idx = np.arange(0, 128, 2, dtype=np.float64) / 128.0
    freq = 1.0 / (2.0 * math.pi) * ROPE["rope_theta"] ** (-idx)
    factor, lo, hi = ROPE["factor"], ROPE["lo_freq_factor"], ROPE["hi_freq_factor"]
    L0 = ROPE["original_context_length"]
    freq_low, freq_high = lo / L0, hi / L0
    freq_scaled = np.where(freq < freq_low, freq / factor, freq)
    smooth = np.clip((L0 * freq - lo) / (hi - lo), 0.0, 1.0)
    freq_smooth = (1.0 - smooth) * (freq / factor) + smooth * freq
    is_mid = (freq >= freq_low) & (freq <= freq_high)
    freq = np.where(is_mid, freq_smooth, freq_scaled)
    phase = 2.0 * math.pi * positions.astype(np.float64)[:, None] * freq[None, :]
    cos = np.cos(phase)
    sin = np.sin(phase)
    cos_t = np.concatenate([cos, cos], axis=1).T
    sin_signed = np.concatenate([-sin, sin], axis=1).T
    return (np.ascontiguousarray(cos_t, np.float32),
            np.ascontiguousarray(sin_signed, np.float32))


def build_masks(cfg, j):
    d = derived(cfg)
    C, CPT, NCH, SA, SBn = d["C"], d["CPT"], d["NCH"], d["SA"], d["SB"]
    kk = np.arange(128)[:, None]
    qq = np.arange(C)[None, :]
    tri = [(qq >= dd * 128 + kk).astype(np.float32) for dd in range(CPT)]
    ones = np.ones((128, C), np.float32)
    zeros = np.zeros((128, C), np.float32)
    m = np.zeros((SBn, 128, C), np.float32)
    for s in range(SA):
        m[s] = ones if s < CPT * j else (
            tri[s - CPT * j] if s < CPT * (j + 1) else zeros)
    cB = NCH - 1 - j
    for s in range(SA, SBn):
        m[s] = ones if s < CPT * cB else (
            tri[s - CPT * cB] if s < CPT * (cB + 1) else zeros)
    return m.astype(bfloat16)


def stage_core_inputs(cfg, c, x, ws):
    d = derived(cfg)
    C, NCH = d["C"], d["NCH"]
    b, j = c // (NCH // 2), c % (NCH // 2)
    pos = np.concatenate([np.arange(C * j, C * (j + 1)),
                          np.arange(C * (NCH - 1 - j), C * (NCH - j))])
    x_loc = np.ascontiguousarray(x[b, pos, :], np.float32)
    cos_t, sin_t = rope_tables(pos)
    m = dict(ws)
    m.update(x_loc=x_loc, rope_cos=cos_t, rope_sin=sin_t,
             masks=build_masks(cfg, j))
    return m


def assemble_output(cfg, results):
    d = derived(cfg)
    C, NCH = d["C"], d["NCH"]
    out = np.zeros((cfg["B"], cfg["T"], cfg["D"]), np.float32)
    for c in range(cfg["B"] * (NCH // 2)):
        b, j = c // (NCH // 2), c % (NCH // 2)
        y = results[c]["y_loc"]
        out[b, C * j:C * (j + 1), :] = y[0:C]
        out[b, C * (NCH - 1 - j):C * (NCH - j), :] = y[C:2 * C]
    return out


def kernel(L0_attn_resid, L0_attn_Wq, L0_attn_Wk, L0_attn_Wv, L0_attn_Wo,
           L0_ffn_gate_w, L0_ffn_base_w, L0_ffn_base_b, L0_ffn_lora_A,
           L0_ffn_lora_B):
    cfg = full_cfg()
    ws = stage_weights(cfg, np.asarray(L0_attn_Wq), np.asarray(L0_attn_Wk),
                       np.asarray(L0_attn_Wv), np.asarray(L0_attn_Wo),
                       np.asarray(L0_ffn_gate_w), np.asarray(L0_ffn_base_w),
                       np.asarray(L0_ffn_base_b), np.asarray(L0_ffn_lora_A),
                       np.asarray(L0_ffn_lora_B))
    x = np.asarray(L0_attn_resid)
    in_maps = [stage_core_inputs(cfg, c, x, ws) for c in range(8)]
    nc = build_program(cfg)
    res = run_bass_kernel_spmd(nc, in_maps, core_ids=list(range(8)))
    return assemble_output(cfg, res.results)
